# version 02
# baseline (speedup 1.0000x reference)
import numpy as np
import ml_dtypes
import concourse.bass as bass
import concourse.mybir as mybir
from concourse.bass import IndirectOffsetOnAxis
from concourse.bass_utils import run_bass_kernel_spmd

F32 = mybir.dt.float32
BF16 = mybir.dt.bfloat16
I32 = mybir.dt.int32
AF = mybir.ActivationFunctionType
ALU = mybir.AluOpType
AX = mybir.AxisListType

B, T, D = 2, 8192, 1024
NCORE = 8
TPC = (B * T) // NCORE
NT = TPC // 128
MAGIC = 12582912.0

_bf16 = ml_dtypes.bfloat16


def _nc():
    return bass.Bass("TRN2", target_bir_lowering=False, debug=False,
                     detect_race_conditions=False)


class P:

    def __init__(self, nc):
        self.nc = nc

    def sb(self, name, shape, dt=F32):
        return self.nc.alloc_sbuf_tensor(name, shape, dt)

    def ps(self, name, shape, dt=F32):
        return self.nc.alloc_psum_tensor(name, shape, dt)

    def sem(self, name):
        return self.nc.alloc_semaphore(name)


class Sched:

    ENGS = ("sync", "gpsimd", "tensor", "scalar", "vector")

    def __init__(self, p):
        self.p = p
        self.sems = {}

    def build(self, fn, block):
        for name in self.ENGS:
            deco = getattr(block, name)

            def body(eng, name=name):
                self.cnt = {}
                self.name = name
                self.eng = eng
                fn(self)
            deco(body)
        self.name = None
        self.eng = None

    def sem(self, name):
        if name not in self.sems:
            self.sems[name] = self.p.sem(name)
        return self.sems[name]

    def val(self, name):
        return self.cnt.get(name, 0)

    def do(self, engname, fn, wait=(), inc=None, inck=1):
        if inc is not None:
            self.cnt[inc] = self.cnt.get(inc, 0) + inck
        if self.name == engname:
            for sn, v in wait:
                if v > 0:
                    self.eng.wait_ge(self.sem(sn), v)
            ins = fn(self.eng)
            if inc is not None:
                ins.then_inc(self.sem(inc), inck)
        return self.cnt.get(inc, 0) if inc is not None else None

    def wait(self, engname, *waits):
        if self.name == engname:
            for sn, v in waits:
                if v > 0:
                    self.eng.wait_ge(self.sem(sn), v)



def build_ln(eps=1e-5, residual=False):
    nc = _nc()
    p = P(nc)
    x = nc.dram_tensor("x", [TPC, D], F32, kind="ExternalInput").ap()
    if residual:
        a_tm = nc.dram_tensor("a_tm", [TPC, D], F32, kind="ExternalInput").ap()
        f_tm = nc.dram_tensor("f_tm", [TPC, D], F32, kind="ExternalInput").ap()
        x1_tm = nc.dram_tensor("x1_tm", [TPC, D], F32, kind="ExternalOutput").ap()
    xn_tm = nc.dram_tensor("xn_tm", [TPC, D], F32, kind="ExternalOutput").ap()
    xn_fm32 = nc.dram_tensor("xn_fm32", [D, TPC], F32, kind="ExternalOutput").ap()
    xn_fm = nc.dram_tensor("xn_fm", [D, TPC], BF16, kind="ExternalOutput").ap()

    eye = nc.inline_tensor(np.eye(128, dtype=np.float32), name="eye128f")

    xt = [p.sb(f"xt{i}", [128, D]) for i in range(4)]
    if residual:
        at_ = [p.sb(f"at{i}", [128, D]) for i in range(4)]
        ft_ = [p.sb(f"ft{i}", [128, D]) for i in range(4)]
    sq = p.sb("sq", [128, D])
    sums = p.sb("sums", [128, NT])
    sqs = p.sb("sqs", [128, NT])
    mu = p.sb("mu", [128, NT])
    m2 = p.sb("m2", [128, NT])
    e2 = p.sb("e2", [128, NT])
    var = p.sb("var", [128, NT])
    rstd = p.sb("rstd", [128, NT])
    xnt = [p.sb(f"xnt{i}", [128, D]) for i in range(2)]
    fm32 = [p.sb(f"fm32_{j}", [128, TPC]) for j in range(8)]
    fmb = [p.sb(f"fmb{j}", [128, TPC], BF16) for j in range(8)]
    eyet = p.sb("eyet", [128, 128])
    tp = [p.ps(f"tp{i}", [128, 128]) for i in range(2)]

    ld = [p.sem(f"ld{i}") for i in range(4)]
    ld2 = [p.sem(f"ld2_{i}") for i in range(4)]
    res_v = p.sem("res_v")
    res_st = p.sem("res_st")
    eld = p.sem("eld")
    sc = p.sem("sc")
    vr = p.sem("vr")
    vt = p.sem("vt")
    v1 = p.sem("v1")
    s2 = p.sem("s2")
    v2 = p.sem("v2")
    tpd = p.sem("tpd")
    tpc_ = p.sem("tpc")
    tpb = p.sem("tpb")
    st_tm = [p.sem("st_tm0"), p.sem("st_tm1")]
    st_fm = p.sem("st_fm")

    block = nc.Block()
    block.__enter__()

    nload = 3 if residual else 1

    @block.sync
    def _(sync):
        sync.dma_start(out=eyet[:, :], in_=eye[:, :]).then_inc(eld, 16)
        for i in range(NT):
            q = i % 4
            if i >= 4:
                sync.wait_ge(ld[q], 16 * nload * (i // 4))
                sync.wait_ge(sc, i - 3)
                sync.wait_ge(vr, i - 3)
                if residual:
                    sync.wait_ge(res_v, i - 3)
                    sync.wait_ge(res_st, 16 * (i - 3))
            sync.dma_start(out=xt[q][:, :], in_=x[i * 128:(i + 1) * 128, :]).then_inc(ld[q], 16)
            if residual:
                sync.dma_start(out=at_[q][:, :], in_=a_tm[i * 128:(i + 1) * 128, :]).then_inc(ld[q], 16)
                sync.dma_start(out=ft_[q][:, :], in_=f_tm[i * 128:(i + 1) * 128, :]).then_inc(ld[q], 16)
        src2 = x1_tm if residual else x
        for i in range(NT):
            q = i % 4
            if i >= 4:
                sync.wait_ge(ld2[q], 16 * (i // 4))
                sync.wait_ge(v2, i - 3)
            else:
                sync.wait_ge(ld[q], 16 * nload * (NT // 4))
                sync.wait_ge(vr, NT)
                sync.wait_ge(sc, NT)
            if residual:
                sync.wait_ge(res_st, 16 * (i + 1))
            sync.dma_start(out=xt[q][:, :], in_=src2[i * 128:(i + 1) * 128, :]).then_inc(ld2[q], 16)

    @block.scalar
    def _(scalar):
        for i in range(NT):
            scalar.wait_ge(ld[i % 4], 16 * nload * (i // 4 + 1))
            if residual:
                scalar.wait_ge(res_v, i + 1)
            scalar.activation(out=sq[:, :], in_=xt[i % 4][:, :], func=AF.Square,
                              accum_out=sqs[:, i:i + 1]).then_inc(sc, 1)
        scalar.wait_ge(v1, 1)
        scalar.activation(out=var[:, :], in_=var[:, :], func=AF.Ln).then_inc(s2, 1)
        scalar.wait_ge(s2, 1)
        scalar.activation(out=rstd[:, :], in_=var[:, :], func=AF.Exp,
                          scale=-0.5).then_inc(s2, 1)
        for g in range(NT * 8):
            scalar.wait_ge(tpc_, g + 1)
            i, j = g // 8, g % 8
            scalar.copy(out=fmb[j][:, i * 128:(i + 1) * 128],
                        in_=fm32[j][:, i * 128:(i + 1) * 128]).then_inc(tpb, 1)

    @block.vector
    def _(vector):
        for i in range(NT):
            vector.wait_ge(ld[i % 4], 16 * nload * (i // 4 + 1))
            if residual:
                vector.tensor_tensor(out=xt[i % 4][:, :], in0=xt[i % 4][:, :],
                                     in1=at_[i % 4][:, :], op=ALU.add)
                vector.tensor_tensor(out=xt[i % 4][:, :], in0=xt[i % 4][:, :],
                                     in1=ft_[i % 4][:, :], op=ALU.add).then_inc(res_v, 1)
            vector.reduce_sum(out=sums[:, i:i + 1], in_=xt[i % 4][:, :], axis=AX.X,
                              negate=False).then_inc(vr, 1)
        vector.wait_ge(vr, NT)
        vector.wait_ge(sc, NT)
        vector.tensor_scalar(out=mu[:, :], in0=sums[:, :], scalar1=1.0 / D,
                             scalar2=None, op0=ALU.mult).then_inc(vt, 1)
        vector.tensor_scalar(out=e2[:, :], in0=sqs[:, :], scalar1=1.0 / D,
                             scalar2=float(eps), op0=ALU.mult, op1=ALU.add).then_inc(vt, 1)
        vector.wait_ge(vt, 2)
        vector.tensor_tensor(out=m2[:, :], in0=mu[:, :], in1=mu[:, :], op=ALU.mult).then_inc(vt, 1)
        vector.wait_ge(vt, 3)
        vector.tensor_tensor(out=var[:, :], in0=e2[:, :], in1=m2[:, :],
                             op=ALU.subtract).then_inc(v1, 1)
        vector.wait_ge(s2, 2)
        for i in range(NT):
            b = i % 2
            vector.wait_ge(ld2[i % 4], 16 * (i // 4 + 1))
            if i >= 2:
                vector.wait_ge(st_tm[b], 16 * (i // 2))
                vector.wait_ge(tpd, 8 * (i - 1))
            vector.tensor_scalar(out=xnt[b][:, :], in0=xt[i % 4][:, :], scalar1=mu[:, i:i + 1],
                                 scalar2=rstd[:, i:i + 1], op0=ALU.subtract,
                                 op1=ALU.mult).then_inc(v2, 1)
            for j in range(8):
                g = i * 8 + j
                vector.wait_ge(tpd, g + 1)
                vector.tensor_copy(out=fm32[j][:, i * 128:(i + 1) * 128],
                                   in_=tp[g % 2][:, :]).then_inc(tpc_, 1)

    @block.tensor
    def _(tensor):
        tensor.wait_ge(eld, 16)
        for i in range(NT):
            b = i % 2
            tensor.wait_ge(v2, i + 1)
            for j in range(8):
                g = i * 8 + j
                if g >= 2:
                    tensor.wait_ge(tpc_, g - 1)
                tensor.transpose(tp[g % 2][:, :], xnt[b][:, j * 128:(j + 1) * 128],
                                 eyet[:, :]).then_inc(tpd, 1)

    @block.gpsimd
    def _(g):
        if residual:
            for i in range(NT):
                g.wait_ge(res_v, i + 1)
                g.dma_start(out=x1_tm[i * 128:(i + 1) * 128, :],
                            in_=xt[i % 4][:, :]).then_inc(res_st, 16)
        for i in range(NT):
            b = i % 2
            g.wait_ge(v2, i + 1)
            g.dma_start(out=xn_tm[i * 128:(i + 1) * 128, :], in_=xnt[b][:, :]).then_inc(st_tm[b], 16)
        g.wait_ge(tpc_, NT * 8)
        g.wait_ge(tpb, NT * 8)
        for j in range(8):
            g.dma_start(out=xn_fm32[j * 128:(j + 1) * 128, :], in_=fm32[j][:, :]).then_inc(st_fm, 16)
            g.dma_start(out=xn_fm[j * 128:(j + 1) * 128, :], in_=fmb[j][:, :]).then_inc(st_fm, 16)
        g.wait_ge(st_fm, 16 * 16)
        g.wait_ge(st_tm[0], 16 * (NT // 2))
        g.wait_ge(st_tm[1], 16 * (NT // 2))

    block.__exit__(None, None, None)
    return nc



def build_ffn1():
    E = 8
    nc = _nc()
    p = P(nc)
    xn_fm32 = nc.dram_tensor("xn_fm32", [D, TPC], F32, kind="ExternalInput").ap()
    rwt = nc.dram_tensor("rwt", [D, E], F32, kind="ExternalInput").ap()
    rb = nc.dram_tensor("rb", [1, E], F32, kind="ExternalInput").ap()
    ewt = nc.dram_tensor("ewt", [E, D, D], F32, kind="ExternalInput").ap()
    eb = nc.dram_tensor("eb", [E, D], F32, kind="ExternalInput").ap()
    y = nc.dram_tensor("y", [TPC, D], F32, kind="ExternalOutput").ap()

    ones_row = nc.inline_tensor(np.ones((1, 128), dtype=np.float32), name="ones1x128f")
    eye = nc.inline_tensor(np.eye(128, dtype=_bf16), name="eye128")

    xb32 = [p.sb(f"xb32_{i}", [128, TPC]) for i in range(2)]
    xf = [p.sb(f"xf{j}", [128, TPC], BF16) for j in range(8)]
    rwtt = p.sb("rwtt", [128, 8 * E])
    rbt = p.sb("rbt", [1, E])
    ebt = p.sb("ebt", [E, D], BF16)
    onr = p.sb("onr", [1, 128])
    eyet = p.sb("eyet", [128, 128], BF16)
    rb_bc = p.sb("rb_bc", [128, E])
    web = [p.sb(f"web{i}", [128, 8 * D], BF16) for i in range(2)]
    lg = [p.sb(f"lg{i}", [128, E]) for i in range(2)]
    prob = p.sb("prob", [128, E])
    nmx = p.sb("nmx", [128, 1])
    sume = p.sb("sume", [128, 1])
    wrec = p.sb("wrec", [128, 1])
    m8 = p.sb("m8", [128, E])
    minv = p.sb("minv", [128, E])
    np8 = p.sb("np8", [128, E])
    zer8 = p.sb("zer8", [128, E])
    sel8 = p.sb("sel8", [128, E])
    zall = p.sb("zall", [128, NT * E])
    zbf = [p.sb(f"zbf{i}", [128, E], BF16) for i in range(2)]
    ztr_sb = [p.sb(f"ztr_sb{i}", [E, 128], BF16) for i in range(2)]
    acc = [p.sb(f"acc{i}", [128, D]) for i in range(NT)]

    lgacc = p.sb("lgacc", [128, NT * E])
    ps_lg = [p.ps(f"ps_lg{i}", [128, 512]) for i in range(2)]
    ps_y = [p.ps(f"ps_y{i}", [128, D]) for i in range(2)]
    ps_zt = [p.ps(f"ps_zt{i}", [E, 128], BF16) for i in range(2)]

    lx = [p.sem("lx0"), p.sem("lx1")]
    ldc = p.sem("ldc")
    ldg = p.sem("ldg")
    wld = [p.sem("wld0"), p.sem("wld1")]
    xc = p.sem("xc")
    lgj = p.sem("lgj")
    lga = p.sem("lga")
    rbm = p.sem("rbm")
    rbc = p.sem("rbc")
    rt = p.sem("rt")
    rs = p.sem("rs")
    zc = p.sem("zc")
    ztd = p.sem("ztd")
    ztc = p.sem("ztc")
    bmm = p.sem("bmm")
    rdone = p.sem("rdone")
    mmv = p.sem("mmv")
    ev = p.sem("ev")
    sto = p.sem("sto")

    NTE = NT * E
    block = nc.Block()
    block.__enter__()

    @block.sync
    def _(sync):
        sync.dma_start(out=onr[:, :], in_=ones_row[:, :]).then_inc(ldc, 16)
        sync.dma_start(out=eyet[:, :], in_=eye[:, :]).then_inc(ldc, 16)
        for j in range(8):
            sync.dma_start(out=rwtt[:, j * E:(j + 1) * E],
                           in_=rwt[j * 128:(j + 1) * 128, :]).then_inc(ldc, 16)
        sync.dma_start(out=rbt[:, :], in_=rb[:, :]).then_inc(ldc, 16)
        for j in range(8):
            q = j % 2
            if j >= 2:
                sync.wait_ge(lx[q], 16 * (j // 2))
                sync.wait_ge(xc, j - 1)
                sync.wait_ge(lgj, j - 1)
            sync.dma_start(out=xb32[q][:, :], in_=xn_fm32[j * 128:(j + 1) * 128, :]).then_inc(lx[q], 16)
        sync.wait_ge(ev, NTE)
        for i in range(NT):
            sync.dma_start(out=y[i * 128:(i + 1) * 128, :], in_=acc[i][:, :]).then_inc(sto, 16)
        sync.wait_ge(sto, 16 * NT)

    @block.gpsimd
    def _(g):
        g.dma_start(out=ebt[:, :], in_=eb[:, :]).then_inc(ldg, 16)
        for e in range(E):
            par = e % 2
            if e >= 2:
                g.wait_ge(wld[par], 16 * 8 * (e // 2))
                g.wait_ge(mmv, (e - 1) * NT)
            for j in range(8):
                g.dma_start(out=web[par][:, j * D:(j + 1) * D],
                            in_=ewt[e, j * 128:(j + 1) * 128, :]).then_inc(wld[par], 16)

    @block.scalar
    def _(scalar):
        for j in range(8):
            q = j % 2
            scalar.wait_ge(lx[q], 16 * (j // 2 + 1))
            scalar.copy(out=xf[j][:, :], in_=xb32[q][:, :]).then_inc(xc, 1)
        for i in range(NT):
            bk = i % 2
            scalar.wait_ge(rt, i * 8 + 2)
            scalar.activation(out=prob[:, :], in_=lg[bk][:, :], func=AF.Exp,
                              bias=nmx[:, :], accum_out=sume[:, :]).then_inc(rs, 1)

    @block.tensor
    def _(tensor):
        tensor.wait_ge(ldc, 16 * 11)
        tensor.matmul(ps_lg[0][:, 200:200 + E], onr[:, :], rbt[:, :],
                      start=True, stop=True).then_inc(rbm, 1)
        for j in range(8):
            q = j % 2
            tensor.wait_ge(lx[q], 16 * (j // 2 + 1))
            if j == 0:
                tensor.wait_ge(rbc, 1)
            if j >= 2:
                tensor.wait_ge(lga, j - 1)
            for i in range(NT):
                ins = tensor.matmul(ps_lg[j % 2][:, i * E:(i + 1) * E],
                                    xb32[q][:, i * 128:(i + 1) * 128],
                                    rwtt[:, j * E:(j + 1) * E],
                                    start=True, stop=True)
            ins.then_inc(lgj, 1)
        for i in range(NT):
            bk = i % 2
            tensor.wait_ge(zc, i + 1)
            if i >= 2:
                tensor.wait_ge(ztc, i - 1)
            tensor.transpose(ps_zt[bk][:, :], zbf[bk][:, :], eyet[:, :]).then_inc(ztd, 1)
            tensor.wait_ge(ztc, i + 1)
            if i >= 2:
                tensor.wait_ge(rdone, i - 1)
            tensor.matmul(ps_y[bk][:, 0:512], ztr_sb[bk][:, :], ebt[:, 0:512],
                          start=True, stop=True)
            tensor.matmul(ps_y[bk][:, 512:1024], ztr_sb[bk][:, :], ebt[:, 512:1024],
                          start=True, stop=True).then_inc(bmm, 1)
        tensor.wait_ge(rdone, NT)
        for e in range(E):
            par = e % 2
            tensor.wait_ge(wld[par], 16 * 8 * (e // 2 + 1))
            for i in range(NT):
                gidx = e * NT + i
                pb = gidx % 2
                if gidx >= 2:
                    tensor.wait_ge(ev, gidx - 1)
                last = None
                for half in range(2):
                    for j in range(8):
                        last = tensor.matmul(ps_y[pb][:, half * 512:(half + 1) * 512],
                                             xf[j][:, i * 128:(i + 1) * 128],
                                             web[par][:, j * D + half * 512:j * D + (half + 1) * 512],
                                             start=(j == 0), stop=(j == 7))
                last.then_inc(mmv, 1)

    @block.vector
    def _(vector):
        tcount = [0]

        def tick(ins):
            ins.then_inc(rt, 1)
            tcount[0] += 1
            vector.wait_ge(rt, tcount[0])

        vector.memset(zer8[:, :], 0.0)
        vector.wait_ge(rbm, 1)
        vector.tensor_copy(out=rb_bc[:, :], in_=ps_lg[0][:, 200:200 + E]).then_inc(rbc, 1)
        for j in range(8):
            vector.wait_ge(lgj, j + 1)
            if j == 0:
                vector.tensor_copy(out=lgacc[:, :],
                                   in_=ps_lg[0][:, 0:NT * E]).then_inc(lga, 1)
            else:
                vector.tensor_tensor(out=lgacc[:, :], in0=lgacc[:, :],
                                     in1=ps_lg[j % 2][:, 0:NT * E],
                                     op=ALU.add).then_inc(lga, 1)
        vector.wait_ge(ldg, 16)
        for i in range(NT):
            bk = i % 2
            tick(vector.tensor_tensor(out=lg[bk][:, :], in0=lgacc[:, i * E:(i + 1) * E],
                                      in1=rb_bc[:, :], op=ALU.add))
            tick(vector.reduce_max(out=nmx[:, :], in_=lg[bk][:, :], axis=AX.X, negate=True))
            tick(vector.tensor_scalar(out=m8[:, :], in0=lg[bk][:, :], scalar1=nmx[:, :],
                                      scalar2=0.0, op0=ALU.add, op1=ALU.is_ge))
            tick(vector.tensor_scalar(out=minv[:, :], in0=lg[bk][:, :], scalar1=nmx[:, :],
                                      scalar2=0.0, op0=ALU.add, op1=ALU.is_lt))
            tick(vector.tensor_tensor_scan(out=np8[:, :], data0=minv[:, :], data1=zer8[:, :],
                                           initial=1.0, op0=ALU.mult, op1=ALU.add))
            vector.tensor_copy(out=sel8[:, 0:1], in_=m8[:, 0:1])
            tick(vector.tensor_tensor(out=sel8[:, 1:E], in0=m8[:, 1:E], in1=np8[:, 0:E - 1],
                                      op=ALU.mult))
            vector.wait_ge(rs, i + 1)
            tick(vector.reciprocal(out=wrec[:, :], in_=sume[:, :]))
            tick(vector.tensor_scalar(out=zall[:, i * E:(i + 1) * E], in0=sel8[:, :],
                                      scalar1=wrec[:, :], scalar2=None, op0=ALU.mult))
            vector.tensor_copy(out=zbf[bk][:, :],
                               in_=zall[:, i * E:(i + 1) * E]).then_inc(zc, 1)
            tcount[0] += 0
            vector.wait_ge(ztd, i + 1)
            vector.tensor_copy(out=ztr_sb[bk][:, :], in_=ps_zt[bk][:, :]).then_inc(ztc, 1)
            vector.wait_ge(bmm, i + 1)
            vector.tensor_copy(out=acc[i][:, :], in_=ps_y[bk][:, :]).then_inc(rdone, 1)
        for e in range(E):
            for i in range(NT):
                gidx = e * NT + i
                pb = gidx % 2
                vector.wait_ge(mmv, gidx + 1)
                vector.scalar_tensor_tensor(out=acc[i][:, :], in0=ps_y[pb][:, :],
                                            scalar=zall[:, i * E + e:i * E + e + 1],
                                            in1=acc[i][:, :], op0=ALU.mult,
                                            op1=ALU.add).then_inc(ev, 1)

    block.__exit__(None, None, None)
    return nc



def build_qkvc():
    nc = _nc()
    p = P(nc)
    xn_fm32 = nc.dram_tensor("xn_fm32", [D, TPC], F32, kind="ExternalInput").ap()
    wq = nc.dram_tensor("wq", [D, D], F32, kind="ExternalInput").ap()
    wk = nc.dram_tensor("wk", [D, D], F32, kind="ExternalInput").ap()
    wv = nc.dram_tensor("wv", [D, D], F32, kind="ExternalInput").ap()
    wd = nc.dram_tensor("wd", [D, 1], F32, kind="ExternalInput").ap()
    bdc = nc.dram_tensor("bdc", [128, 1], F32, kind="ExternalInput").ap()
    oscol_d = nc.dram_tensor("oscol", [128, 1], F32, kind="ExternalInput").ap()
    q_fm = nc.dram_tensor("q_fm", [D, TPC], BF16, kind="ExternalOutput").ap()
    k_fm = nc.dram_tensor("k_fm", [D, TPC], BF16, kind="ExternalOutput").ap()
    v_tm = nc.dram_tensor("v_tm", [TPC, D], BF16, kind="ExternalOutput").ap()
    ccol = nc.dram_tensor("ccol", [128, NT], F32, kind="ExternalOutput").ap()

    xb32 = [p.sb(f"xb32_{i}", [128, TPC]) for i in range(2)]
    xnb = [p.sb(f"xnb{j}", [128, TPC], BF16) for j in range(8)]
    wqt = p.sb("wqt", [128, 8 * D], BF16)
    wkt = p.sb("wkt", [128, 8 * D], BF16)
    wvt = p.sb("wvt", [128, 8 * D], BF16)
    wdt = p.sb("wdt", [128, 8])
    bdt = p.sb("bdt", [128, 1])
    osd = p.sb("osd", [128, 1])
    ob = [p.sb(f"ob{i}", [128, 512], BF16) for i in range(4)]
    cc = p.sb("cc", [128, NT])
    cc2 = p.sb("cc2", [128, NT])
    cca = p.sb("cca", [128, NT])

    ps_p = [p.ps(f"ps_p{i}", [128, 512]) for i in range(4)]
    ps_c = [p.ps(f"ps_c{i}", [128, NT]) for i in range(2)]
    ccacc = None

    lx = [p.sem("lx0"), p.sem("lx1")]
    ldc = p.sem("ldc")
    ldg = p.sem("ldg")
    xc = p.sem("xc")
    dmm = p.sem("dmm")
    dca = p.sem("dca")
    dcs = p.sem("dcs")
    dcv = p.sem("dcv")
    pm = p.sem("pm")
    pe_q = p.sem("pe_q")
    pe_kv = p.sem("pe_kv")
    sto = p.sem("sto")

    block = nc.Block()
    block.__enter__()

    groups = []
    for e in range(8):
        for cch in range(4):
            groups.append(("q", e, cch))
    for e in range(8):
        for cch in range(4):
            groups.append(("k", e, cch))
    for tt in range(16):
        for hh in range(2):
            groups.append(("v", tt, hh))
    NG = len(groups)

    @block.sync
    def _(sync):
        sync.dma_start(out=bdt[:, :], in_=bdc[:, :]).then_inc(ldc, 16)
        sync.dma_start(out=osd[:, :], in_=oscol_d[:, :]).then_inc(ldc, 16)
        for j in range(8):
            sync.dma_start(out=wdt[:, j:j + 1], in_=wd[j * 128:(j + 1) * 128, :]).then_inc(ldc, 16)
        for j in range(8):
            qq = j % 2
            if j >= 2:
                sync.wait_ge(lx[qq], 16 * (j // 2))
                sync.wait_ge(xc, j - 1)
                sync.wait_ge(dmm, j - 1)
            sync.dma_start(out=xb32[qq][:, :], in_=xn_fm32[j * 128:(j + 1) * 128, :]).then_inc(lx[qq], 16)

    @block.gpsimd
    def _(g):
        for j in range(8):
            g.dma_start(out=wqt[:, j * D:(j + 1) * D], in_=wq[j * 128:(j + 1) * 128, :]).then_inc(ldg, 16)
        for j in range(8):
            g.dma_start(out=wkt[:, j * D:(j + 1) * D], in_=wk[j * 128:(j + 1) * 128, :]).then_inc(ldg, 16)
        for j in range(8):
            g.dma_start(out=wvt[:, j * D:(j + 1) * D], in_=wv[j * 128:(j + 1) * 128, :]).then_inc(ldg, 16)
        nq = 0
        nkv = 0
        for gi, (kind, a, b_) in enumerate(groups):
            if kind == "q":
                nq += 1
                g.wait_ge(pe_q, nq)
            else:
                nkv += 1
                g.wait_ge(pe_kv, nkv)
            src = ob[gi % 4][:, :]
            if kind == "q":
                dst = q_fm[a * 128:(a + 1) * 128, b_ * 512:(b_ + 1) * 512]
            elif kind == "k":
                dst = k_fm[a * 128:(a + 1) * 128, b_ * 512:(b_ + 1) * 512]
            else:
                dst = v_tm[a * 128:(a + 1) * 128, b_ * 512:(b_ + 1) * 512]
            g.dma_start(out=dst, in_=src).then_inc(sto, 16)
        g.wait_ge(dcv, 2)
        g.dma_start(out=ccol[:, :], in_=cc2[:, :]).then_inc(sto, 16)
        g.wait_ge(sto, 16 * (NG + 1))

    @block.scalar
    def _(scalar):
        for j in range(8):
            qq = j % 2
            scalar.wait_ge(lx[qq], 16 * (j // 2 + 1))
            scalar.copy(out=xnb[j][:, :], in_=xb32[qq][:, :]).then_inc(xc, 1)
        scalar.wait_ge(dca, 9)
        scalar.activation(out=cc[:, :], in_=cca[:, :], func=AF.Sigmoid,
                          bias=bdt[:, :]).then_inc(dcs, 1)
        scalar.wait_ge(dcv, 1)
        scalar.activation(out=cc2[:, :], in_=cc2[:, :], func=AF.Ln).then_inc(dcs, 2)
        for gi, (kind, a, b_) in enumerate(groups):
            if kind != "q":
                continue
            scalar.wait_ge(dca, 1)
            scalar.wait_ge(pm, gi + 1)
            if gi >= 4:
                scalar.wait_ge(sto, 16 * (gi - 3))
            scalar.activation(out=ob[gi % 4][:, :], in_=ps_p[gi % 4][:, :], func=AF.Copy,
                              scale=osd[:, :]).then_inc(pe_q, 1)

    @block.vector
    def _(vector):
        vector.wait_ge(ldc, 16 * 10)
        vector.tensor_scalar(out=osd[:, :], in0=osd[:, :], scalar1=float(1.0 / np.sqrt(D)),
                             scalar2=None, op0=ALU.mult).then_inc(dca, 1)
        for j in range(8):
            vector.wait_ge(dmm, j + 1)
            if j == 0:
                vector.tensor_copy(out=cca[:, :], in_=ps_c[0][:, :]).then_inc(dca, 1)
            else:
                vector.tensor_tensor(out=cca[:, :], in0=cca[:, :], in1=ps_c[j % 2][:, :],
                                     op=ALU.add).then_inc(dca, 1)
        vector.wait_ge(dcs, 1)
        vector.tensor_scalar(out=cc2[:, :], in0=cc[:, :], scalar1=1e-8, scalar2=None,
                             op0=ALU.add).then_inc(dcv, 1)
        vector.wait_ge(dcs, 3)
        vector.sem_inc(dcv, 1)
        for gi, (kind, a, b_) in enumerate(groups):
            if kind == "q":
                continue
            vector.wait_ge(pm, gi + 1)
            if gi >= 4:
                vector.wait_ge(sto, 16 * (gi - 3))
            vector.tensor_copy(out=ob[gi % 4][:, :], in_=ps_p[gi % 4][:, :]).then_inc(pe_kv, 1)

    @block.tensor
    def _(tensor):
        tensor.wait_ge(ldc, 16 * 10)
        for j in range(8):
            qq = j % 2
            tensor.wait_ge(lx[qq], 16 * (j // 2 + 1))
            if j >= 2:
                tensor.wait_ge(dca, j)
            for tt in range(16):
                ins = tensor.matmul(ps_c[j % 2][:, tt:tt + 1], xb32[qq][:, tt * 128:(tt + 1) * 128],
                                    wdt[:, j:j + 1], start=True, stop=True)
            ins.then_inc(dmm, 1)
        tensor.wait_ge(ldg, 16 * 24)
        tensor.wait_ge(xc, 8)
        nq_at = []
        nkv_at = []
        cq = ckv = 0
        for (kind, _, _) in groups:
            if kind == "q":
                cq += 1
            else:
                ckv += 1
            nq_at.append(cq)
            nkv_at.append(ckv)
        for gi, (kind, a, b_) in enumerate(groups):
            if gi >= 4:
                pk, _, _ = groups[gi - 4]
                if pk == "q":
                    tensor.wait_ge(pe_q, nq_at[gi - 4])
                else:
                    tensor.wait_ge(pe_kv, nkv_at[gi - 4])
            last = None
            for j in range(8):
                if kind == "q":
                    lhsT = wqt[:, j * D + a * 128:j * D + (a + 1) * 128]
                    rhs = xnb[j][:, b_ * 512:(b_ + 1) * 512]
                elif kind == "k":
                    lhsT = wkt[:, j * D + a * 128:j * D + (a + 1) * 128]
                    rhs = xnb[j][:, b_ * 512:(b_ + 1) * 512]
                else:
                    lhsT = xnb[j][:, a * 128:(a + 1) * 128]
                    rhs = wvt[:, j * D + b_ * 512:j * D + (b_ + 1) * 512]
                last = tensor.matmul(ps_p[gi % 4][:, :], lhsT, rhs,
                                     start=(j == 0), stop=(j == 7))
            last.then_inc(pm, 1)

    block.__exit__(None, None, None)
    return nc



def build_attn0():
    NQS = 16
    nc = _nc()
    p = P(nc)
    qf = nc.dram_tensor("qf", [D, TPC], BF16, kind="ExternalInput").ap()
    trow_d = nc.dram_tensor("trow", [1, TPC], F32, kind="ExternalInput").ap()
    kf = nc.dram_tensor("kf", [D, T], BF16, kind="ExternalInput").ap()
    vt = nc.dram_tensor("vt", [T, D], BF16, kind="ExternalInput").ap()
    ccol_d = nc.dram_tensor("ccol", [128, 64], F32, kind="ExternalInput").ap()
    wo = nc.dram_tensor("wo", [D, D], F32, kind="ExternalInput").ap()
    attn_fm = nc.dram_tensor("attn_fm", [D, TPC], F32, kind="ExternalOutput").ap()

    ones_row = nc.inline_tensor(np.ones((1, 128), dtype=np.float32), name="ones1x128f")
    eye = nc.inline_tensor(np.eye(128, dtype=np.float32), name="eye128f")

    qt = [p.sb(f"qt{j}", [128, TPC], BF16) for j in range(8)]
    trowt = p.sb("trowt", [1, TPC])
    onr = p.sb("onr", [1, 128])
    eyet = p.sb("eyet", [128, 128])
    ccolt = p.sb("ccolt", [128, 64])
    negc = p.sb("negc", [128, 64])
    scol = p.sb("scol", [128, 64])
    kblk = [p.sb(f"kblk{i}", [128, 8 * 1024], BF16) for i in range(2)]
    vblk = [p.sb(f"vblk{i}", [128, 8 * 1024], BF16) for i in range(2)]
    tbc = [p.sb(f"tbc{i}", [128, 128]) for i in range(NQS)]
    argt = p.sb("argt", [128, 128])
    mskt = p.sb("mskt", [128, 128])
    wtile = p.sb("wtile", [128, 128])
    wm = p.sb("wm", [128, 128])
    swb = [p.sb(f"swb{i}", [128, 128], BF16) for i in range(2)]
    racc = [p.sb(f"racc{i}", [128, D]) for i in range(8)]
    rfm = [p.sb(f"rfm{j}", [128, TPC], BF16) for j in range(8)]
    wot = p.sb("wot", [128, 8 * D], BF16)
    ob = [p.sb(f"ob{i}", [128, 512]) for i in range(4)]

    ps_s = [p.ps(f"ps_s{i}", [128, 128]) for i in range(2)]
    ps_r = [p.ps(f"ps_r{i}", [128, D]) for i in range(2)]
    ps_t = [p.ps(f"ps_t{i}", [128, 128]) for i in range(2)]

    ldc = p.sem("ldc")
    ldg = p.sem("ldg")
    lkb = [p.sem("lkb0"), p.sem("lkb1")]
    tbm = p.sem("tbm")
    tbe = p.sem("tbe")
    iot = p.sem("iot")
    sm = p.sem("sm")
    av = p.sem("av")
    se = p.sem("se")
    swe = p.sem("swe")
    rtv = p.sem("rtv")
    rvm = p.sem("rvm")
    rve = p.sem("rve")
    epi = p.sem("epi")
    tpd = p.sem("tpd")
    tpc_ = p.sem("tpc")
    rt = p.sem("rt")
    wom = p.sem("wom")
    woe = p.sem("woe")
    sto = p.sem("sto")

    blocks = []
    for kbb in range(8):
        blocks.append((0, kbb))
    for kbb in range(4, 8):
        blocks.append((1, kbb))
    pairs = []
    groups = []
    for bl, (half, kbb) in enumerate(blocks):
        for qs_ in range(8):
            qs = half * 8 + qs_
            jq = qs // 4
            if 16 * jq > kbb * 8 + 7:
                continue
            gi = len(groups)
            groups.append((bl, qs))
            for kbl in range(8):
                pairs.append((bl, qs, kbl, kbb // 2 == jq, kbl == 0, kbl == 7, gi))
    NP = len(pairs)
    NGRP = len(groups)
    first_g = {}
    last_g = {}
    for gi, (bl, qs) in enumerate(groups):
        if qs not in first_g:
            first_g[qs] = gi
        last_g[qs] = gi
    epi_order = sorted(range(NQS), key=lambda q: last_g[q])
    epi_rank = {q: r for r, q in enumerate(epi_order)}

    block = nc.Block()
    block.__enter__()

    @block.sync
    def _(sync):
        for j in range(8):
            sync.dma_start(out=qt[j][:, :], in_=qf[j * 128:(j + 1) * 128, :]).then_inc(ldc, 16)
        sync.dma_start(out=trowt[:, :], in_=trow_d[:, :]).then_inc(ldc, 16)
        sync.dma_start(out=onr[:, :], in_=ones_row[:, :]).then_inc(ldc, 16)
        sync.dma_start(out=eyet[:, :], in_=eye[:, :]).then_inc(ldc, 16)
        sync.dma_start(out=ccolt[:, :], in_=ccol_d[:, :]).then_inc(ldc, 16)
        for bl, (half, kbb) in enumerate(blocks):
            par = bl % 2
            if bl >= 2:
                sync.wait_ge(lkb[par], 256 * (bl // 2))
                lp = max(i for i, pr in enumerate(pairs) if pr[0] == bl - 2)
                sync.wait_ge(rtv, lp + 1)
            for j in range(8):
                sync.dma_start(out=kblk[par][:, j * 1024:(j + 1) * 1024],
                               in_=kf[j * 128:(j + 1) * 128, kbb * 1024:(kbb + 1) * 1024]).then_inc(lkb[par], 16)
            for i in range(8):
                sync.dma_start(out=vblk[par][:, i * 1024:(i + 1) * 1024],
                               in_=vt[kbb * 1024 + i * 128:kbb * 1024 + (i + 1) * 128, :]).then_inc(lkb[par], 16)
        for j in range(8):
            for cch in range(4):
                oi = j * 4 + cch
                sync.wait_ge(woe, oi + 1)
                sync.dma_start(out=attn_fm[j * 128:(j + 1) * 128, cch * 512:(cch + 1) * 512],
                               in_=ob[oi % 4][:, :]).then_inc(sto, 16)
        sync.wait_ge(sto, 16 * 32)

    @block.gpsimd
    def _(g):
        for j in range(8):
            g.dma_start(out=wot[:, j * D:(j + 1) * D], in_=wo[j * 128:(j + 1) * 128, :]).then_inc(ldg, 16)
        g.wait_ge(ldc, 16 * 12)
        g.iota(scol[:, :], pattern=[[128, 64]], base=-1, channel_multiplier=1,
               allow_small_or_imprecise_dtypes=True).then_inc(iot, 1)

    @block.tensor
    def _(tensor):
        tensor.wait_ge(ldc, 16 * 12)
        for qs in range(NQS):
            bk = qs % 2
            if qs >= 2:
                tensor.wait_ge(tbe, qs - 1)
            tensor.matmul(ps_s[bk][:, :], onr[:, :], trowt[0:1, qs * 128:(qs + 1) * 128],
                          start=True, stop=True).then_inc(tbm, 1)
        for pi in range(NP + 1):
            if pi < NP:
                bl, qs, kbl, masked, firstk, lastk, gi = pairs[pi]
                par = bl % 2
                tensor.wait_ge(lkb[par], 256 * (bl // 2 + 1))
                if pi == 0:
                    tensor.wait_ge(tbe, NQS)
                if pi >= 2:
                    tensor.wait_ge(swe, pi - 1)
                for j in range(8):
                    ins = tensor.matmul(ps_s[pi % 2][:, :],
                                        kblk[par][:, j * 1024 + kbl * 128:j * 1024 + (kbl + 1) * 128],
                                        qt[j][:, qs * 128:(qs + 1) * 128],
                                        start=(j == 0), stop=(j == 7))
                ins.then_inc(sm, 1)
            if pi >= 1:
                bl, qs, kbl, masked, firstk, lastk, gi = pairs[pi - 1]
                par = bl % 2
                tensor.wait_ge(swe, pi)
                if firstk and gi >= 2:
                    tensor.wait_ge(rve, gi - 1)
                for half in range(2):
                    ins = tensor.matmul(ps_r[gi % 2][:, half * 512:(half + 1) * 512],
                                        swb[(pi - 1) % 2][:, :],
                                        vblk[par][:, kbl * 1024 + half * 512:kbl * 1024 + (half + 1) * 512],
                                        start=firstk, stop=lastk)
                ins.then_inc(rtv, 1)
                if lastk and gi == last_g[qs]:
                    r = epi_rank[qs]
                    tensor.wait_ge(rve, gi + 1)
                    for j in range(8):
                        g2 = r * 8 + j
                        if g2 >= 2:
                            tensor.wait_ge(tpc_, g2 - 1)
                        tensor.transpose(ps_t[g2 % 2][:, :],
                                         racc[qs % 8][:, j * 128:(j + 1) * 128],
                                         eyet[:, :]).then_inc(tpd, 1)
        tensor.wait_ge(ldg, 16 * 8)
        tensor.wait_ge(tpc_, NQS * 8)
        for j in range(8):
            for cch in range(4):
                oi = j * 4 + cch
                if oi >= 4:
                    tensor.wait_ge(woe, oi - 3)
                for jj in range(8):
                    ins = tensor.matmul(ps_r[oi % 2][:, 0:512],
                                        wot[:, jj * D + j * 128:jj * D + (j + 1) * 128],
                                        rfm[jj][:, cch * 512:(cch + 1) * 512],
                                        start=(jj == 0), stop=(jj == 7))
                ins.then_inc(wom, 1)

    @block.scalar
    def _(scalar):
        for pi in range(NP):
            scalar.wait_ge(av, pi + 1)
            scalar.activation(out=wtile[:, :], in_=argt[:, :], func=AF.Exp).then_inc(se, 1)

    @block.vector
    def _(vector):
        vector.wait_ge(iot, 1)
        vector.tensor_scalar(out=negc[:, :], in0=ccolt[:, :], scalar1=-1.0, scalar2=None,
                             op0=ALU.mult)
        for qs in range(NQS):
            bk = qs % 2
            vector.wait_ge(tbm, qs + 1)
            vector.tensor_copy(out=tbc[qs][:, :], in_=ps_s[bk][:, :]).then_inc(tbe, 1)
        tk = [0]
        for pi in range(NP):
            bl, qs, kbl, masked, firstk, lastk, gi = pairs[pi]
            half, kbb = blocks[bl]
            kb = kbb * 8 + kbl
            if pi >= 1:
                vector.wait_ge(se, pi)
            if masked:
                vector.tensor_scalar(out=argt[:, :], in0=tbc[qs][:, :], scalar1=scol[:, kb:kb + 1],
                                     scalar2=negc[:, kb:kb + 1], op0=ALU.subtract, op1=ALU.mult)
                vector.tensor_scalar(out=argt[:, :], in0=argt[:, :], scalar1=1.0,
                                     scalar2=None, op0=ALU.min).then_inc(rt, 1)
                tk[0] += 1
                vector.wait_ge(rt, tk[0])
                vector.tensor_scalar(out=mskt[:, :], in0=tbc[qs][:, :], scalar1=scol[:, kb:kb + 1],
                                     scalar2=1.0, op0=ALU.subtract, op1=ALU.is_lt).then_inc(av, 1)
            else:
                vector.tensor_scalar(out=argt[:, :], in0=tbc[qs][:, :], scalar1=scol[:, kb:kb + 1],
                                     scalar2=negc[:, kb:kb + 1], op0=ALU.subtract,
                                     op1=ALU.mult).then_inc(av, 1)
            vector.wait_ge(se, pi + 1)
            vector.wait_ge(sm, pi + 1)
            if pi >= 2:
                vector.wait_ge(rtv, pi - 1)
            if masked:
                vector.tensor_tensor(out=wm[:, :], in0=wtile[:, :], in1=mskt[:, :],
                                     op=ALU.mult).then_inc(rt, 1)
                tk[0] += 1
                vector.wait_ge(rt, tk[0])
                vector.tensor_tensor(out=swb[pi % 2][:, :], in0=ps_s[pi % 2][:, :],
                                     in1=wm[:, :], op=ALU.mult).then_inc(swe, 1)
            else:
                vector.tensor_tensor(out=swb[pi % 2][:, :], in0=ps_s[pi % 2][:, :],
                                     in1=wtile[:, :], op=ALU.mult).then_inc(swe, 1)
            if lastk:
                vector.wait_ge(rtv, pi + 1)
                if gi == first_g[qs]:
                    vector.tensor_copy(out=racc[qs % 8][:, :],
                                       in_=ps_r[gi % 2][:, :]).then_inc(rve, 1)
                else:
                    vector.tensor_tensor(out=racc[qs % 8][:, :], in0=racc[qs % 8][:, :],
                                         in1=ps_r[gi % 2][:, :], op=ALU.add).then_inc(rve, 1)
                if gi == last_g[qs]:
                    r = epi_rank[qs]
                    for j in range(8):
                        g2 = r * 8 + j
                        vector.wait_ge(tpd, g2 + 1)
                        vector.tensor_copy(out=rfm[j][:, qs * 128:(qs + 1) * 128],
                                           in_=ps_t[g2 % 2][:, :]).then_inc(tpc_, 1)
        for oi in range(32):
            vector.wait_ge(wom, oi + 1)
            if oi >= 4:
                vector.wait_ge(sto, 16 * (oi - 3))
            vector.tensor_copy(out=ob[oi % 4][:, :], in_=ps_r[oi % 2][:, 0:512]).then_inc(woe, 1)

    block.__exit__(None, None, None)
    return nc



def build_add3():
    nc = _nc()
    p = P(nc)
    x = nc.dram_tensor("x", [TPC, D], F32, kind="ExternalInput").ap()
    a_tm = nc.dram_tensor("a_tm", [TPC, D], F32, kind="ExternalInput").ap()
    f_tm = nc.dram_tensor("f_tm", [TPC, D], F32, kind="ExternalInput").ap()
    out = nc.dram_tensor("out", [TPC, D], F32, kind="ExternalOutput").ap()

    xt = [p.sb(f"xt{i}", [128, D]) for i in range(4)]
    at_ = [p.sb(f"at{i}", [128, D]) for i in range(4)]
    ft_ = [p.sb(f"ft{i}", [128, D]) for i in range(4)]
    ld = [p.sem(f"ld{i}") for i in range(4)]
    va = p.sem("va")
    sto = p.sem("sto")

    block = nc.Block()
    block.__enter__()

    @block.sync
    def _(sync):
        for i in range(NT):
            q = i % 4
            if i >= 4:
                sync.wait_ge(ld[q], 48 * (i // 4))
                sync.wait_ge(sto, 16 * (i - 3))
            sync.dma_start(out=xt[q][:, :], in_=x[i * 128:(i + 1) * 128, :]).then_inc(ld[q], 16)
            sync.dma_start(out=at_[q][:, :], in_=a_tm[i * 128:(i + 1) * 128, :]).then_inc(ld[q], 16)
            sync.dma_start(out=ft_[q][:, :], in_=f_tm[i * 128:(i + 1) * 128, :]).then_inc(ld[q], 16)
        sync.wait_ge(sto, 16 * NT)

    @block.vector
    def _(vector):
        for i in range(NT):
            q = i % 4
            vector.wait_ge(ld[q], 48 * (i // 4 + 1))
            vector.tensor_tensor(out=xt[q][:, :], in0=xt[q][:, :], in1=at_[q][:, :], op=ALU.add)
            vector.tensor_tensor(out=xt[q][:, :], in0=xt[q][:, :], in1=ft_[q][:, :],
                                 op=ALU.add).then_inc(va, 1)

    @block.gpsimd
    def _(g):
        for i in range(NT):
            g.wait_ge(va, i + 1)
            g.dma_start(out=out[i * 128:(i + 1) * 128, :], in_=xt[i % 4][:, :]).then_inc(sto, 16)
        g.wait_ge(sto, 16 * NT)

    block.__exit__(None, None, None)
    return nc



def build_castw():
    E0, FF = 16, 4096
    nc = _nc()
    w1 = nc.dram_tensor("w1", [E0, D, FF], F32, kind="ExternalInput").ap()
    w2 = nc.dram_tensor("w2", [E0, FF, D], F32, kind="ExternalInput").ap()
    w1b = nc.dram_tensor("w1b", [E0 * D, FF], BF16, kind="ExternalOutput").ap()
    w2b = nc.dram_tensor("w2b", [E0 * FF, D], BF16, kind="ExternalOutput").ap()
    sto = nc.alloc_semaphore("sto")
    block = nc.Block()
    block.__enter__()

    @block.gpsimd
    def _(g):
        n = 0
        for e in range(E0):
            for j in range(8):
                g.dma_start(out=w1b[e * D + j * 128:e * D + (j + 1) * 128, :],
                            in_=w1[e, j * 128:(j + 1) * 128, :]).then_inc(sto, 16)
                n += 1
            for j in range(8):
                g.dma_start(out=w2b[e * FF + j * 512:e * FF + (j + 1) * 512, :],
                            in_=w2[e, j * 512:(j + 1) * 512, :]).then_inc(sto, 16)
                n += 1
        g.wait_ge(sto, 16 * n)

    block.__exit__(None, None, None)
    return nc



def build_ffn0(nlayers=4, debug=False):
    E0, FF, KS = 16, 4096, 256
    NSLOT = TPC // KS + (E0 - 1)
    NL = nlayers
    nc = _nc()
    p = P(nc)
    xn_tm = nc.dram_tensor("xn_tm", [TPC, D], F32, kind="ExternalInput").ap()
    routers = nc.dram_tensor("routers", [4, D, E0], F32, kind="ExternalInput").ap()
    w1b = nc.dram_tensor("w1b", [E0 * D, FF], BF16, kind="ExternalInput").ap()
    w2b = nc.dram_tensor("w2b", [E0 * FF, D], BF16, kind="ExternalInput").ap()
    ffn_tm = nc.dram_tensor("ffn_tm", [TPC, D], F32, kind="ExternalOutput").ap()
    kind_dbg = "ExternalOutput" if debug else "Internal"
    xsrt = nc.dram_tensor("xsrt", [NSLOT * KS, D], BF16, kind=kind_dbg).ap()
    ysrt = [nc.dram_tensor(f"ysrt{i}", [NSLOT * KS, D], F32, kind=kind_dbg).ap()
            for i in range(2)]
    if debug:
        dbg_o = nc.dram_tensor("dbg_o", [128, NT * E0], F32, kind="ExternalOutput").ap()
        dbg_r = nc.dram_tensor("dbg_r", [128, NT * E0], F32, kind="ExternalOutput").ap()
        dbg_b = nc.dram_tensor("dbg_b", [1, NT * E0], F32, kind="ExternalOutput").ap()
        dbg_p = nc.dram_tensor("dbg_p", [1, E0], F32, kind="ExternalOutput").ap()
        dbg_e = nc.dram_tensor("dbg_e", [1, 128], F32, kind="ExternalOutput").ap()
        dbg_d = nc.dram_tensor("dbg_d", [128, NT], I32, kind="ExternalOutput").ap()
        dbg_z = nc.dram_tensor("dbg_z", [128, NT], F32, kind="ExternalOutput").ap()
        dbg_w1 = nc.dram_tensor("dbg_w1", [128, NSLOT * 8], I32, kind="ExternalOutput").ap()
        dbg_hf = nc.dram_tensor("dbg_hf", [128, 32 * KS], BF16, kind="ExternalOutput").ap()
        dbg_xf = nc.dram_tensor("dbg_xf", [128, 8 * KS], BF16, kind="ExternalOutput").ap()
        dbg_w1s = nc.dram_tensor("dbg_w1s", [128, 8 * 512], BF16, kind="ExternalOutput").ap()
        dbg_w2s = nc.dram_tensor("dbg_w2s", [128, 4 * D], BF16, kind="ExternalOutput").ap()
        dbg_w2i = nc.dram_tensor("dbg_w2i", [128, NSLOT * 32], I32, kind="ExternalOutput").ap()
        dbg_w2r = nc.dram_tensor("dbg_w2r", [1, NSLOT * 32], F32, kind="ExternalOutput").ap()

    tri_su = nc.inline_tensor(np.triu(np.ones((128, 128), np.float32), 1), name="trisu")
    ones_col = nc.inline_tensor(np.ones((128, 1), np.float32), name="ones128x1")
    ones_row = nc.inline_tensor(np.ones((1, 128), np.float32), name="ones1x128f")
    eyef = nc.inline_tensor(np.eye(128, dtype=np.float32), name="eye128f")
    eyeb = nc.inline_tensor(np.eye(128, dtype=_bf16), name="eye128")

    xt = [p.sb(f"xt{i}", [128, D]) for i in range(NT)]
    xbfs = p.sb("xbfs", [128, NT * D], BF16)
    rut = p.sb("rut", [128, 8 * E0])
    trsb = [p.sb(f"trsb{i}", [128, 128]) for i in range(2)]
    trisu_t = p.sb("trisu_t", [128, 128])
    onc = p.sb("onc", [128, 1])
    onrw = p.sb("onrw", [1, 128])
    eyeft = p.sb("eyeft", [128, 128])
    eyebt = p.sb("eyebt", [128, 128], BF16)
    lg = [p.sb(f"lg{i}", [128, E0]) for i in range(2)]
    prob = p.sb("prob", [128, E0])
    nmx = p.sb("nmx", [128, 1])
    sume = p.sb("sume", [128, 1])
    wrec = p.sb("wrec", [128, 1])
    m16 = p.sb("m16", [128, E0])
    minv = p.sb("minv", [128, E0])
    np16 = p.sb("np16", [128, E0])
    zer16 = p.sb("zer16", [128, E0])
    one16 = p.sb("one16", [1, E0])
    zc_ = [p.sb(f"zc{i}", [128, NT]) for i in range(2)]
    Osb = p.sb("Osb", [128, NT * E0])
    ranksb = p.sb("ranksb", [128, NT * E0])
    basesb = p.sb("basesb", [1, NT * E0])
    totrow = p.sb("totrow", [1, E0])
    base_acc = p.sb("base_acc", [1, E0])
    prw = [p.sb(f"prw{i}", [1, E0]) for i in range(3)]
    pb = p.sb("pb", [128, E0])
    slotcol = p.sb("slotcol", [128, 1])
    iotap = p.sb("iotap", [128, 1])
    cmp16 = p.sb("cmp16", [128, E0])
    eidf = p.sb("eidf", [128, 1])
    eidrow = p.sb("eidrow", [1, 128])
    iotaj1 = p.sb("iotaj1", [1, NSLOT * 8])
    iotaj2 = p.sb("iotaj2", [1, NSLOT * 32])
    w1r = p.sb("w1r", [1, NSLOT * 8])
    w2r = p.sb("w2r", [1, NSLOT * 32])
    widx1 = p.sb("widx1", [128, NSLOT * 8], I32)
    widx2 = p.sb("widx2", [128, NSLOT * 32], I32)
    s3t = p.sb("s3t", [128, E0])
    mm16 = p.sb("mm16", [128, E0])
    destf = p.sb("destf", [128, 1])
    desti = [p.sb(f"desti{i}", [128, NT], I32) for i in range(2)]
    xss = [p.sb(f"xss{i}", [128, 2 * D], BF16) for i in range(2)]
    xsfm = [p.sb(f"xsfm{i}", [128, 8 * KS], BF16) for i in range(2)]
    w1s = [p.sb(f"w1s{i}", [128, 8 * 512], BF16) for i in range(2)]
    w2s = [p.sb(f"w2s{i}", [128, 4 * D], BF16) for i in range(2)]
    hfm = p.sb("hfm", [128, 32 * KS], BF16)
    ysb = [p.sb(f"ysb{i}", [128, 2 * D]) for i in range(2)]
    zrow = p.sb("zrow", [128, D], BF16)

    tp2 = [p.ps(f"tp2_{i}", [128, 512]) for i in range(2)]
    lgh = [p.ps(f"lgh{i}", [128, 512]) for i in range(2)]
    yp2 = [p.ps(f"yp2_{i}", [128, D]) for i in range(2)]

    sch = Sched(p)
    lu = {}

    def emit(s):
        V, S_, T, G, SY = "vector", "scalar", "tensor", "gpsimd", "sync"
        lu.clear()

        def tick(fn, wait=()):
            n = s.do(V, fn, wait=wait, inc="vt")
            s.wait(V, ("vt", n))
            return n

        s.do(SY, lambda e: e.dma_start(out=trisu_t[:, :], in_=tri_su[:, :]), inc="ldc", inck=16)
        s.do(SY, lambda e: e.dma_start(out=onc[:, :], in_=ones_col[:, :]), inc="ldc", inck=16)
        s.do(SY, lambda e: e.dma_start(out=onrw[:, :], in_=ones_row[:, :]), inc="ldc", inck=16)
        s.do(SY, lambda e: e.dma_start(out=eyeft[:, :], in_=eyef[:, :]), inc="ldc", inck=16)
        s.do(SY, lambda e: e.dma_start(out=eyebt[:, :], in_=eyeb[:, :]), inc="ldc", inck=16)
        ldc_all = s.val("ldc")
        s.do(V, lambda e: e.memset(zer16[:, :], 0.0), inc="vt")
        s.do(V, lambda e: e.memset(one16[:, :], 1.0), inc="vt")
        s.do(G, lambda e: e.iota(slotcol[:, :], pattern=[[1, 1]], base=0,
                                 channel_multiplier=KS,
                                 allow_small_or_imprecise_dtypes=True), inc="gio")
        s.do(G, lambda e: e.iota(iotap[:, :], pattern=[[1, 1]], base=0,
                                 channel_multiplier=1,
                                 allow_small_or_imprecise_dtypes=True), inc="gio")
        s.do(G, lambda e: e.iota(iotaj1.ap().rearrange("o (a j) -> o a j", j=8),
                                 pattern=[[0, NSLOT], [128, 8]], base=0,
                                 channel_multiplier=0,
                                 allow_small_or_imprecise_dtypes=True), inc="gio")
        s.do(G, lambda e: e.iota(iotaj2.ap().rearrange("o (a j) -> o a j", j=32),
                                 pattern=[[0, NSLOT], [128, 32]], base=0,
                                 channel_multiplier=0,
                                 allow_small_or_imprecise_dtypes=True), inc="gio")
        gio_all = s.val("gio")
        s.do(V, lambda e: e.memset(zrow[:, :], 0.0), inc="vt")
        zr_v = s.val("vt")
        for r0 in range(0, NSLOT * KS, 128):
            s.do(G, lambda e, r0=r0: e.dma_start(out=xsrt[r0:r0 + 128, :], in_=zrow[:, :]),
                 wait=([("vt", zr_v)] if r0 == 0 else ()), inc="dsc", inck=16)

        for l in range(NL):
            w_prev_lg = [lu.get("lgm_layer", ("lgp", 0))]
            for j in range(8):
                s.do(SY, lambda e, l=l, j=j: e.dma_start(
                    out=rut[:, j * E0:(j + 1) * E0],
                    in_=routers[l, j * 128:(j + 1) * 128, :]),
                    wait=w_prev_lg if j == 0 else (), inc="ldr", inck=16)
            ldr_now = s.val("ldr")

            for i in range(NT):
                gi = l * NT + i
                wx = [("tpd", lu.get(f"xt{i}", 0)), ("xcc", lu.get(f"xbf{i}", 0)),
                      ("dsc", lu.get(f"xbf_sc{i}", 0))]
                if l == 0:
                    s.do(SY, lambda e, i=i: e.dma_start(
                        out=xt[i][:, :], in_=xn_tm[i * 128:(i + 1) * 128, :]),
                        wait=wx, inc=f"gx{i % 2}", inck=16)
                    gxn = f"gx{i % 2}"
                else:
                    s.do(G, lambda e, i=i, l=l: e.indirect_dma_start(
                        out=xt[i][:, :], out_offset=None,
                        in_=ysrt[(l - 1) % 2][:, :],
                        in_offset=IndirectOffsetOnAxis(desti[(l - 1) % 2][:, i:i + 1], axis=0)),
                        wait=wx + [("yst", lu["yst_layer"])], inc=f"ggx{i % 2}", inck=16)
                    gxn = f"ggx{i % 2}"
                gxv = s.val(gxn)
                if l > 0:
                    s.do(V, lambda e, i=i, l=l: e.tensor_scalar(
                        out=xt[i][:, :], in0=xt[i][:, :],
                        scalar1=zc_[(l - 1) % 2][:, i:i + 1], scalar2=None, op0=ALU.mult),
                        wait=[(gxn, gxv)], inc="xsc")
                    wx_fin = [("xsc", s.val("xsc"))]
                else:
                    wx_fin = [(gxn, gxv)]
                s.do(S_, lambda e, i=i: e.copy(out=xbfs[:, i * D:(i + 1) * D], in_=xt[i][:, :]),
                     wait=wx_fin, inc="xcc")
                lu[f"xbf{i}"] = s.val("xcc")
                wlg_bank = [lu.get(f"lgbank{gi % 2}", ("vt", 0)), ("hse", lu.get("hse_layer", 0))]
                for j in range(8):
                    gt = gi * 8 + j
                    s.do(T, lambda e, i=i, j=j, gt=gt: e.transpose(
                        tp2[gt % 2][:, 0:128], xt[i][:, j * 128:(j + 1) * 128], eyeft[:, :]),
                        wait=wx_fin + [("ldc", ldc_all), lu.get(f"tpb{gt % 2}", ("tpe", 0)),
                                       ("xte", lu.get("xte_layer", 0))],
                        inc="tpd")
                    s.do(V, lambda e, j=j, gt=gt: e.tensor_copy(
                        out=trsb[j % 2][:, :], in_=tp2[gt % 2][:, 0:128]),
                        wait=[("tpd", s.val("tpd")), ("lgp", lu.get(f"trs{j % 2}", 0))],
                        inc="tpe")
                    lu[f"tpb{gt % 2}"] = ("tpe", s.val("tpe"))
                    s.do(T, lambda e, j=j, gi=gi: e.matmul(
                        lgh[gi % 2][:, 0:E0], trsb[j % 2][:, :], rut[:, j * E0:(j + 1) * E0],
                        start=(j == 0), stop=(j == 7)),
                        wait=[("tpe", s.val("tpe")), ("ldr", ldr_now)] + (wlg_bank if j == 0 else []),
                        inc="lgp")
                    lu[f"trs{j % 2}"] = s.val("lgp")
                s.do(T, lambda e: e.nop()) if False else None
                lgm_i = s.do(T, None, inc=None) if False else None
                lgp_done = s.val("lgp")
                tick(lambda e, gi=gi: e.tensor_copy(out=lg[gi % 2][:, :], in_=lgh[gi % 2][:, 0:E0]),
                     wait=[("lgp", lgp_done)])
                lu[f"lgbank{gi % 2}"] = ("vt", s.val("vt"))
                tick(lambda e, gi=gi: e.reduce_max(out=nmx[:, :], in_=lg[gi % 2][:, :],
                                                   axis=AX.X, negate=True))
                nmx_v = s.val("vt")
                s.do(S_, lambda e, gi=gi: e.activation(out=prob[:, :], in_=lg[gi % 2][:, :],
                                                       func=AF.Exp, bias=nmx[:, :],
                                                       accum_out=sume[:, :]),
                     wait=[("vt", nmx_v)], inc="rs")
                rs_v = s.val("rs")
                tick(lambda e, gi=gi: e.tensor_scalar(out=m16[:, :], in0=lg[gi % 2][:, :],
                                                      scalar1=nmx[:, :], scalar2=0.0,
                                                      op0=ALU.add, op1=ALU.is_ge))
                tick(lambda e, gi=gi: e.tensor_scalar(out=minv[:, :], in0=lg[gi % 2][:, :],
                                                      scalar1=nmx[:, :], scalar2=0.0,
                                                      op0=ALU.add, op1=ALU.is_lt))
                tick(lambda e: e.tensor_tensor_scan(out=np16[:, :], data0=minv[:, :],
                                                    data1=zer16[:, :], initial=1.0,
                                                    op0=ALU.mult, op1=ALU.add))
                s.do(V, lambda e, i=i: e.tensor_copy(out=Osb[:, i * E0:i * E0 + 1],
                                                     in_=m16[:, 0:1]), inc="vt")
                tick(lambda e, i=i: e.tensor_tensor(out=Osb[:, i * E0 + 1:(i + 1) * E0],
                                                    in0=m16[:, 1:E0], in1=np16[:, 0:E0 - 1],
                                                    op=ALU.mult))
                O_done = s.val("vt")
                tick(lambda e: e.reciprocal(out=wrec[:, :], in_=sume[:, :]),
                     wait=[("rs", rs_v)])
                tick(lambda e, l=l, i=i: e.tensor_copy(out=zc_[l % 2][:, i:i + 1],
                                                       in_=wrec[:, :]))
                s.do(T, lambda e, i=i: e.matmul(yp2[0][:, 0:E0], trisu_t[:, :],
                                                Osb[:, i * E0:(i + 1) * E0],
                                                start=True, stop=True),
                     wait=[("vt", O_done), ("rke", lu.get("rkbank", 0)),
                           ("yev", lu.get("yev_layer", 0))],
                     inc="rkm")
                s.do(V, lambda e, i=i: e.tensor_copy(out=ranksb[:, i * E0:(i + 1) * E0],
                                                     in_=yp2[0][:, 0:E0]),
                     wait=[("rkm", s.val("rkm"))], inc="rke")
                lu["rkbank"] = s.val("rke")
                if i == 0:
                    s.do(V, lambda e: e.memset(base_acc[:, :], 0.0), inc="vt")
                    s.wait(V, ("vt", s.val("vt")))
                s.do(T, lambda e, i=i: e.matmul(yp2[1][0:1, (i % 2) * 512:(i % 2) * 512 + E0],
                                                onc[:, :], Osb[:, i * E0:(i + 1) * E0],
                                                start=True, stop=True),
                     wait=[("vt", O_done), ("bse", lu.get(f"bsb{i % 2}", 0)),
                           ("yev", lu.get("yev_layer", 0))],
                     inc="bsm")
                bsm_v = s.val("bsm")
                tick(lambda e, i=i: e.tensor_copy(out=basesb[0:1, i * E0:(i + 1) * E0],
                                                  in_=base_acc[:, :]))
                n = s.do(V, lambda e, i=i: e.tensor_tensor(
                    out=base_acc[:, :], in0=base_acc[:, :],
                    in1=yp2[1][0:1, (i % 2) * 512:(i % 2) * 512 + E0], op=ALU.add),
                    wait=[("bsm", bsm_v)], inc="bse")
                s.wait(V, ("bse", n))
                lu[f"bsb{i % 2}"] = n
            lu["lgm_layer"] = ("lgp", s.val("lgp"))

            tick(lambda e: e.tensor_copy(out=totrow[:, :], in_=base_acc[:, :]),
                 wait=[("bse", s.val("bse"))])
            tick(lambda e: e.tensor_scalar(out=prw[0][:, :], in0=totrow[:, :],
                                           scalar1=1.0 / KS, scalar2=0.5 - 1.0 / (2 * KS),
                                           op0=ALU.mult, op1=ALU.add))
            tick(lambda e: e.tensor_scalar(out=prw[0][:, :], in0=prw[0][:, :],
                                           scalar1=MAGIC, scalar2=-MAGIC,
                                           op0=ALU.add, op1=ALU.add))
            tick(lambda e: e.tensor_scalar(out=prw[1][:, :], in0=prw[0][:, :],
                                           scalar1=float(KS), scalar2=None, op0=ALU.mult))
            tick(lambda e: e.tensor_tensor_scan(out=prw[2][:, :], data0=one16[:, :],
                                                data1=prw[1][:, :], initial=0.0,
                                                op0=ALU.mult, op1=ALU.add))
            s.do(V, lambda e: e.memset(prw[0][0:1, 0:1], 0.0), inc="vt")
            tick(lambda e: e.tensor_copy(out=prw[0][0:1, 1:E0], in_=prw[2][0:1, 0:E0 - 1]))
            ps_row = s.val("vt")
            s.do(T, lambda e: e.matmul(lgh[0][:, 0:E0], onrw[:, :], prw[0][:, :],
                                       start=True, stop=True),
                 wait=[("vt", ps_row), lu.get("lgbank0", ("vt", 0))], inc="sbm")
            tick(lambda e: e.tensor_copy(out=pb[:, :], in_=lgh[0][:, 0:E0]),
                 wait=[("sbm", s.val("sbm"))])
            lu["lgbank0"] = ("vt", s.val("vt"))
            tick(lambda e: e.tensor_scalar(out=cmp16[:, :], in0=pb[:, :], scalar1=slotcol[:, :],
                                           scalar2=None, op0=ALU.is_le),
                 wait=[("gio", gio_all)])
            tick(lambda e: e.reduce_sum(out=eidf[:, :], in_=cmp16[:, :], axis=AX.X,
                                        negate=False))
            tick(lambda e: e.tensor_scalar(out=eidf[:, :], in0=eidf[:, :], scalar1=-1.0,
                                           scalar2=None, op0=ALU.add))
            eid_v = s.val("vt")
            s.do(T, lambda e: e.transpose(tp2[0][0:1, 0:128], eidf[:, :], eyeft[:, :]),
                 wait=[("vt", eid_v), lu.get("tpb0", ("tpe", 0))], inc="eit")
            tick(lambda e: e.tensor_copy(out=eidrow[:, :], in_=tp2[0][0:1, 0:128]),
                 wait=[("eit", s.val("eit"))])
            lu["tpb0"] = ("vt", s.val("vt"))
            tick(lambda e: e.tensor_scalar(
                out=w1r.ap().rearrange("o (a j) -> o a j", j=8),
                in0=eidrow[0:1, 0:NSLOT].unsqueeze(2).broadcast_to([1, NSLOT, 8]),
                scalar1=float(D), scalar2=None, op0=ALU.mult))
            tick(lambda e: e.tensor_tensor(out=w1r[:, :], in0=w1r[:, :], in1=iotaj1[:, :],
                                           op=ALU.add))
            tick(lambda e: e.tensor_scalar(
                out=w2r.ap().rearrange("o (a j) -> o a j", j=32),
                in0=eidrow[0:1, 0:NSLOT].unsqueeze(2).broadcast_to([1, NSLOT, 32]),
                scalar1=float(FF), scalar2=None, op0=ALU.mult))
            tick(lambda e: e.tensor_tensor(out=w2r[:, :], in0=w2r[:, :], in1=iotaj2[:, :],
                                           op=ALU.add))
            wrow_v = s.val("vt")
            s.do(T, lambda e: e.matmul(tp2[1][:, 0:NSLOT * 8], onrw[:, :], w1r[:, :],
                                       start=True, stop=True),
                 wait=[("vt", wrow_v), lu.get("tpb1", ("tpe", 0))], inc="sbm")
            s.do(T, lambda e: e.matmul(lgh[1][:, 0:512], onrw[:, :], w2r[:, 0:512],
                                       start=True, stop=True),
                 wait=[("vt", wrow_v), lu.get("lgbank1", ("vt", 0))], inc="sbm")
            s.do(T, lambda e: e.matmul(tp2[0][:, 0:NSLOT * 32 - 512], onrw[:, :],
                                       w2r[:, 512:NSLOT * 32], start=True, stop=True),
                 wait=[("vt", wrow_v), ("eit", s.val("eit"))], inc="sbm")
            sbm_v = s.val("sbm")
            tick(lambda e: e.tensor_scalar(out=widx1[:, :],
                                           in0=tp2[1][:, 0:NSLOT * 8], scalar1=iotap[:, :],
                                           scalar2=8.0, op0=ALU.add, op1=ALU.mult),
                 wait=[("sbm", sbm_v)])
            tick(lambda e: e.tensor_scalar(out=widx2[:, 0:512], in0=lgh[1][:, 0:512],
                                           scalar1=iotap[:, :], scalar2=None, op0=ALU.add))
            tick(lambda e: e.tensor_scalar(out=widx2[:, 512:NSLOT * 32],
                                           in0=tp2[0][:, 0:NSLOT * 32 - 512],
                                           scalar1=iotap[:, :], scalar2=None, op0=ALU.add))
            widx_v = s.val("vt")
            lu["tpb1"] = ("vt", widx_v)
            lu["tpb0"] = ("vt", widx_v)
            lu["lgbank1"] = ("vt", widx_v)

            for i in range(NT):
                s.do(T, lambda e, i=i: e.matmul(lgh[i % 2][:, E0:2 * E0], onrw[:, :],
                                                basesb[:, i * E0:(i + 1) * E0],
                                                start=True, stop=True),
                     wait=[("bbe", lu.get(f"bbc{i % 2}", 0)), ("vt", widx_v)], inc="bbm")
                bbm_v = s.val("bbm")
                tick(lambda e, i=i: e.tensor_tensor(out=s3t[:, :],
                                                    in0=ranksb[:, i * E0:(i + 1) * E0],
                                                    in1=pb[:, :], op=ALU.add))
                n = s.do(V, lambda e, i=i: e.tensor_tensor(out=s3t[:, :], in0=s3t[:, :],
                                                           in1=lgh[i % 2][:, E0:2 * E0],
                                                           op=ALU.add),
                         wait=[("bbm", bbm_v), ("vt", s.val("vt"))], inc="bbe")
                s.wait(V, ("bbe", n))
                lu[f"bbc{i % 2}"] = n
                tick(lambda e, i=i: e.tensor_tensor(out=mm16[:, :], in0=s3t[:, :],
                                                    in1=Osb[:, i * E0:(i + 1) * E0],
                                                    op=ALU.mult))
                tick(lambda e: e.reduce_sum(out=destf[:, :], in_=mm16[:, :], axis=AX.X,
                                            negate=False))
                tick(lambda e, l=l, i=i: e.tensor_copy(out=desti[l % 2][:, i:i + 1],
                                                       in_=destf[:, :]))
                dv = s.val("vt")
                s.do(G, lambda e, l=l, i=i: e.indirect_dma_start(
                    out=xsrt[:, :],
                    out_offset=IndirectOffsetOnAxis(desti[l % 2][:, i:i + 1], axis=0),
                    in_=xbfs[:, i * D:(i + 1) * D], in_offset=None),
                    wait=[("vt", dv), ("gxs0", lu.get("slotld0", 0)),
                          ("gxs1", lu.get("slotld1", 0))],
                    inc="dsc", inck=16)
                lu[f"xbf_sc{i}"] = s.val("dsc")
            dsc_layer = s.val("dsc")

            for sl in range(NSLOT):
                gsl = l * NSLOT + sl
                par = sl % 2
                wld_ = [(f"gxs{par}", lu.get(f"gxsu{par}", 0)),
                        ("xtd", lu.get(f"xssu{par}", 0)), ("dsc", dsc_layer)]
                s.do(SY, lambda e, sl=sl: e.dma_start(
                    out=xss[par][:, 0:D], in_=xsrt[sl * KS:sl * KS + 128, :]),
                    wait=wld_, inc=f"gxs{par}", inck=16)
                s.do(SY, lambda e, sl=sl: e.dma_start(
                    out=xss[par][:, D:2 * D], in_=xsrt[sl * KS + 128:(sl + 1) * KS, :]),
                    inc=f"gxs{par}", inck=16)
                gxs_v = s.val(f"gxs{par}")
                lu[f"slotld{par}"] = gxs_v
                for tt in range(16):
                    gx = gsl * 16 + tt
                    ttile, j = tt % 2, tt // 2
                    s.do(T, lambda e, par=par, tt=tt, gx=gx, ttile=ttile, j=j: e.transpose(
                        tp2[gx % 2].ap().bitcast(BF16)[:, 0:128],
                        xss[par][:, ttile * D + j * 128:ttile * D + (j + 1) * 128] if False
                        else xss[par][:, (tt % 2) * D + (tt // 2) * 128:(tt % 2) * D + (tt // 2) * 128 + 128],
                        eyebt[:, :]),
                        wait=[(f"gxs{par}", gxs_v), ("xte", lu.get(f"tpx{gx % 2}", 0)),
                              lu.get(f"tpb{gx % 2}", ("tpe", 0)), ("bbe", s.val("bbe"))],
                        inc="xtd")
                    n = s.do(V, lambda e, par=par, tt=tt, gx=gx: e.tensor_copy(
                        out=xsfm[par][:, (tt // 2) * KS + (tt % 2) * 128:(tt // 2) * KS + (tt % 2) * 128 + 128],
                        in_=tp2[gx % 2].ap().bitcast(BF16)[:, 0:128]),
                        wait=[("xtd", s.val("xtd")), ("hmm", lu.get(f"xsu{par}", 0))],
                        inc="xte")
                    lu[f"tpx{gx % 2}"] = n
                    lu[f"tpb{gx % 2}"] = ("xte", n)
                lu[f"xssu{par}"] = s.val("xtd")
                lu[f"gxsu{par}"] = gxs_v
                xte_slot = s.val("xte")
                for f8 in range(8):
                    gse = gsl * 8 + f8
                    pw = gse % 2
                    for j in range(8):
                        s.do(G, lambda e, pw=pw, sl=sl, f8=f8, j=j: e.indirect_dma_start(
                            out=w1s[pw][:, j * 512:(j + 1) * 512], out_offset=None,
                            in_=w1b.rearrange("r (c k) -> (r c) k", k=512),
                            in_offset=IndirectOffsetOnAxis(widx1[:, sl * 8 + j:sl * 8 + j + 1], axis=0),
                            element_offset=f8 * 512),
                            wait=([("vt", widx_v), (f"gws{pw}", lu.get(f"gwsu{pw}", 0)),
                                   ("hmm", lu.get(f"w1u{pw}", 0)),
                                   ("ymm", lu.get(f"w2u{pw}", 0))] if j == 0 else ()),
                            inc=f"gws{pw}", inck=16)
                    for j2 in range(4):
                        s.do(G, lambda e, pw=pw, sl=sl, f8=f8, j2=j2: e.indirect_dma_start(
                            out=w2s[pw][:, j2 * D:(j2 + 1) * D], out_offset=None,
                            in_=w2b[:, :],
                            in_offset=IndirectOffsetOnAxis(
                                widx2[:, sl * 32 + f8 * 4 + j2:sl * 32 + f8 * 4 + j2 + 1], axis=0)),
                            inc=f"gws{pw}", inck=16)
                    gws_v = s.val(f"gws{pw}")
                    lu[f"gwsu{pw}"] = gws_v
                    for m in range(4):
                        gm = (gsl * 8 + f8) * 4 + m
                        for j in range(8):
                            s.do(T, lambda e, pw=pw, par=par, m=m, j=j: e.matmul(
                                lgh[m % 2][:, (m // 2) * 256:(m // 2) * 256 + KS],
                                w1s[pw][:, j * 512 + m * 128:j * 512 + (m + 1) * 128],
                                xsfm[par][:, j * KS:(j + 1) * KS],
                                start=(j == 0), stop=(j == 7)),
                                wait=([(f"gws{pw}", gws_v), ("xte", xte_slot),
                                       ("hse", lu.get(f"hbank{m % 2}", 0)),
                                       ("lge", s.val("lge")), ("bbe", s.val("bbe"))] if j == 0 else ()),
                                inc="hmm")
                        hmm_v = s.val("hmm")
                        s.do(S_, lambda e, m=m, f8=f8: e.activation(
                            out=hfm[:, (f8 * 4 + m) * KS:(f8 * 4 + m + 1) * KS],
                            in_=lgh[m % 2][:, (m // 2) * 256:(m // 2) * 256 + KS],
                            func=AF.Silu),
                            wait=[("hmm", hmm_v)], inc="hse")
                        lu[f"hbank{m % 2}"] = s.val("hse")
                    lu[f"w1u{pw}"] = s.val("hmm")
                    hse_f8 = s.val("hse")
                    for tt2 in range(2):
                        for half in range(2):
                            for j2 in range(4):
                                s.do(T, lambda e, pw=pw, f8=f8, tt2=tt2, half=half, j2=j2: e.matmul(
                                    yp2[tt2][:, half * 512:(half + 1) * 512],
                                    hfm[:, (f8 * 4 + j2) * KS + tt2 * 128:(f8 * 4 + j2) * KS + (tt2 + 1) * 128],
                                    w2s[pw][:, j2 * D + half * 512:j2 * D + (half + 1) * 512],
                                    start=(f8 == 0 and j2 == 0), stop=(f8 == 7 and j2 == 3)),
                                    wait=([("hse", hse_f8), (f"gws{pw}", gws_v)] +
                                          ([("yev", lu.get("ybanks", 0)),
                                            ("rke", s.val("rke")), ("bsm", s.val("bsm"))]
                                           if (f8 == 0 and tt2 == 0 and half == 0 and j2 == 0) else [])),
                                    inc="ymm")
                    lu[f"w2u{pw}"] = s.val("ymm")
                ymm_slot = s.val("ymm")
                if debug and l == 0 and sl == 0:
                    s.do(SY, lambda e: e.dma_start(out=dbg_hf, in_=hfm[:, :]),
                         wait=[("hse", s.val("hse"))], inc="dbh", inck=16)
                    s.do(SY, lambda e: e.dma_start(out=dbg_xf, in_=xsfm[par][:, :]),
                         wait=[("xte", s.val("xte"))], inc="dbh", inck=16)
                    s.do(SY, lambda e: e.dma_start(out=dbg_w1s, in_=w1s[1][:, :]),
                         wait=[("gws1", s.val("gws1"))], inc="dbh", inck=16)
                    s.do(SY, lambda e: e.dma_start(out=dbg_w2s, in_=w2s[1][:, :]),
                         inc="dbh", inck=16)
                    s.do(SY, lambda e: e.dma_start(out=dbg_w2i, in_=widx2[:, :]),
                         inc="dbh", inck=16)
                    s.do(SY, lambda e: e.dma_start(out=dbg_w2r, in_=w2r[:, :]),
                         inc="dbh", inck=16)
                    lu["dbh"] = s.val("dbh")
                if debug and "dbh" in lu:
                    s.wait("scalar", ("dbh", lu["dbh"]))
                    s.wait("vector", ("dbh", lu["dbh"]))
                    s.wait("gpsimd", ("dbh", lu["dbh"]))
                    del lu["dbh"]
                lu[f"xsu{par}"] = ymm_slot
                par2 = gsl % 2
                for tt2 in range(2):
                    s.do(V, lambda e, tt2=tt2, par2=par2: e.tensor_copy(
                        out=ysb[par2][:, tt2 * D:(tt2 + 1) * D], in_=yp2[tt2][:, :]),
                        wait=[("ymm", ymm_slot), ("yst", lu.get(f"ysbu{par2}", 0))],
                        inc="yev")
                lu["ybanks"] = s.val("yev")
                yev_v = s.val("yev")
                for tt2 in range(2):
                    s.do(SY, lambda e, l=l, sl=sl, tt2=tt2, par2=par2: e.dma_start(
                        out=ysrt[l % 2][sl * KS + tt2 * 128:sl * KS + (tt2 + 1) * 128, :],
                        in_=ysb[par2][:, tt2 * D:(tt2 + 1) * D]),
                        wait=[("yev", yev_v)], inc="yst", inck=16)
                lu[f"ysbu{par2}"] = s.val("yst")
            lu["yst_layer"] = s.val("yst")
            lu["hse_layer"] = s.val("hse")
            lu["xte_layer"] = s.val("xte")
            lu["yev_layer"] = s.val("yev")
            for i in range(NT):
                lu[f"xt{i}"] = s.val("tpd")

        if debug:
            dv_all = s.val("vt")
            s.do(SY, lambda e: e.dma_start(out=dbg_o, in_=Osb[:, :]), wait=[("vt", dv_all)], inc="fst", inck=16)
            s.do(SY, lambda e: e.dma_start(out=dbg_r, in_=ranksb[:, :]), inc="fst", inck=16)
            s.do(SY, lambda e: e.dma_start(out=dbg_b, in_=basesb[:, :]), inc="fst", inck=16)
            s.do(SY, lambda e: e.dma_start(out=dbg_p, in_=prw[0][:, :]), inc="fst", inck=16)
            s.do(SY, lambda e: e.dma_start(out=dbg_e, in_=eidrow[:, :]), inc="fst", inck=16)
            s.do(SY, lambda e: e.dma_start(out=dbg_d, in_=desti[(NL - 1) % 2][:, :]), inc="fst", inck=16)
            s.do(SY, lambda e: e.dma_start(out=dbg_z, in_=zc_[(NL - 1) % 2][:, :]), inc="fst", inck=16)
            s.do(SY, lambda e: e.dma_start(out=dbg_w1, in_=widx1[:, :]), inc="fst", inck=16)
        for i in range(NT):
            s.do(G, lambda e, i=i: e.indirect_dma_start(
                out=xt[i][:, :], out_offset=None,
                in_=ysrt[(NL - 1) % 2][:, :],
                in_offset=IndirectOffsetOnAxis(desti[(NL - 1) % 2][:, i:i + 1], axis=0)),
                wait=[("yst", lu["yst_layer"]), ("tpd", lu.get(f"xt{i}", 0))],
                inc=f"ggx{i % 2}", inck=16)
            gxv = s.val(f"ggx{i % 2}")
            s.do(V, lambda e, i=i: e.tensor_scalar(
                out=xt[i][:, :], in0=xt[i][:, :],
                scalar1=zc_[(NL - 1) % 2][:, i:i + 1], scalar2=None, op0=ALU.mult),
                wait=[(f"ggx{i % 2}", gxv)], inc="xsc")
            xsc_v = s.val("xsc")
            s.do(SY, lambda e, i=i: e.dma_start(
                out=ffn_tm[i * 128:(i + 1) * 128, :], in_=xt[i][:, :]),
                wait=[("xsc", xsc_v)], inc="fst", inck=16)
        s.wait("sync", ("fst", s.val("fst")))
        s.wait("gpsimd", ("fst", s.val("fst")))

    block = nc.Block()
    block.__enter__()
    sch.build(emit, block)
    block.__exit__(None, None, None)
    return nc



CTRL = 6688
H, HD = 16, 64
PI = float(np.pi)


def _mkW(s):
    names = ("tk", "dm", "gld")

    def W():
        return [(n, s.val(n)) for n in names]

    V_ = lambda fn: s.do("vector", fn, wait=W(), inc="tk")
    K_ = lambda fn: s.do("scalar", fn, wait=W(), inc="tk")
    M_ = lambda fn: s.do("tensor", fn, wait=W(), inc="tk")
    L_ = lambda fn: s.do("sync", fn, wait=W(), inc="dm", inck=16)
    S_ = lambda fn: s.do("sync", fn, wait=W(), inc="dm", inck=16)
    G_ = lambda fn: s.do("gpsimd", fn, wait=W(), inc="gld", inck=16)
    return V_, K_, M_, L_, S_, G_


def build_attn1a(debug=False):
    nc = _nc()
    p = P(nc)
    xn_fm = nc.dram_tensor("xn_fm", [D, TPC], BF16, kind="ExternalInput").ap()
    xpre = nc.dram_tensor("xpre", [D, 3], BF16, kind="ExternalInput").ap()
    mpre = nc.dram_tensor("mpre", [128, 3], F32, kind="ExternalInput").ap()
    wct = nc.dram_tensor("wct", [D, CTRL], F32, kind="ExternalInput").ap()
    bct = nc.dram_tensor("bct", [128, 53], F32, kind="ExternalInput").ap()
    qcw = nc.dram_tensor("qcw", [D, 4], F32, kind="ExternalInput").ap()
    kcw = nc.dram_tensor("kcw", [D, 4], F32, kind="ExternalInput").ap()
    vcw = nc.dram_tensor("vcw", [D, 4], F32, kind="ExternalInput").ap()
    wout = nc.dram_tensor("wout", [2 * D, D], F32, kind="ExternalInput").ap()

    qfs = nc.dram_tensor("qfs", [D, TPC], BF16, kind="ExternalOutput").ap()
    kfs = nc.dram_tensor("kfs", [D, TPC], BF16, kind="ExternalOutput").ap()
    ktms = nc.dram_tensor("ktms", [TPC, D], BF16, kind="ExternalOutput").ap()
    vtms = nc.dram_tensor("vtms", [TPC, D], BF16, kind="ExternalOutput").ap()
    Lr16 = nc.dram_tensor("Lr16", [16, TPC], F32, kind="ExternalOutput").ap()
    nRC16 = nc.dram_tensor("nRC16", [16, TPC], F32, kind="ExternalOutput").ap()
    bex16 = nc.dram_tensor("bex16", [16, TPC], F32, kind="ExternalOutput").ap()
    lplb16 = nc.dram_tensor("lplb16", [16, TPC], F32, kind="ExternalOutput").ap()
    eLc16 = nc.dram_tensor("eLc16", [16, 16], F32, kind="ExternalOutput").ap()
    u1s = nc.dram_tensor("u1s", [512, TPC], F32, kind="ExternalOutput").ap()
    u2s = nc.dram_tensor("u2s", [512, TPC], F32, kind="ExternalOutput").ap()
    rc1s = nc.dram_tensor("rc1s", [512, TPC], F32, kind="ExternalOutput").ap()
    rc2s = nc.dram_tensor("rc2s", [512, TPC], F32, kind="ExternalOutput").ap()
    cass = nc.dram_tensor("cass", [512, TPC], F32, kind="ExternalOutput").ap()
    sass = nc.dram_tensor("sass", [512, TPC], F32, kind="ExternalOutput").ap()
    gate_s = nc.dram_tensor("gate_s", [D, TPC], BF16, kind="ExternalOutput").ap()
    wout_b = nc.dram_tensor("wout_b", [2 * D, D], BF16, kind="ExternalOutput").ap()
    kdbg = "ExternalOutput" if debug else "Internal"
    ctrl_scr = nc.dram_tensor("ctrl_scr", [CTRL, TPC], F32, kind=kdbg).ap()
    wct_b = nc.dram_tensor("wct_b", [D, CTRL], BF16, kind="Internal").ap()
    csil = nc.dram_tensor("csil", [D, TPC], F32, kind="Internal").ap()

    ones128 = nc.inline_tensor(np.ones((128, 128), np.float32), name="ones128x128")
    eyeb = nc.inline_tensor(np.eye(128, dtype=_bf16), name="eye128")
    blk = np.zeros((128, 2), np.float32)
    blk[0:64, 0] = 1.0
    blk[64:128, 1] = 1.0
    blk64 = nc.inline_tensor(blk, name="blk64")
    blk64T = nc.inline_tensor(np.ascontiguousarray(blk.T), name="blk64T")

    ones128t = p.sb("ones128t", [128, 128])
    eyebt = p.sb("eyebt", [128, 128], BF16)
    blk64t = p.sb("blk64t", [128, 2])
    blk64Tt = p.sb("blk64Tt", [2, 128])
    xnf = p.sb("xnf", [128, 8 * TPC], BF16)
    xpre_sb = p.sb("xpre_sb", [128, 24], BF16)
    mpret = p.sb("mpret", [128, 3])
    qkvpre = p.sb("qkvpre", [128, 72])
    t3 = p.sb("t3", [128, 3])
    bctt = p.sb("bctt", [128, 53])
    wctb = [p.sb(f"wctb{i}", [128, 8 * 128], BF16) for i in range(2)]
    evb = [p.sb(f"evb{i}", [128, TPC]) for i in range(2)]
    dv0 = p.sb("dv0", [128, TPC])
    dv1 = p.sb("dv1", [128, TPC])
    dv2 = p.sb("dv2", [128, TPC])
    dpad = p.sb("dpad", [128, TPC + 4])
    vstg = p.sb("vstg", [128, TPC], BF16)
    tstg = [p.sb(f"tstg{i}", [128, 128], BF16) for i in range(2)]
    cwt = p.sb("cwt", [128, 4])
    ab2 = p.sb("ab2", [16, TPC])
    be2 = p.sb("be2", [16, TPC])
    la2 = p.sb("la2", [16, TPC])
    lb2 = p.sb("lb2", [16, TPC])
    bex2 = p.sb("bex2", [16, TPC])
    L2 = p.sb("L2", [16, TPC])
    nRC2 = p.sb("nRC2", [16, TPC])
    lp2 = p.sb("lp2", [16, TPC])
    eLc = p.sb("eLc", [16, 16])
    pp = [p.ps(f"pp{i}", [128, 512]) for i in range(8)]

    sch = Sched(p)

    def emit(s):
        V_, K_, M_, L_, S_, G_ = _mkW(s)

        G_(lambda e: e.dma_start(out=wct_b[:, :], in_=wct[:, :]))
        G_(lambda e: e.dma_start(out=wout_b[:, :], in_=wout[:, :]))
        L_(lambda e: e.dma_start(out=ones128t[:, :], in_=ones128[:, :]))
        L_(lambda e: e.dma_start(out=eyebt[:, :], in_=eyeb[:, :]))
        L_(lambda e: e.dma_start(out=blk64t[:, :], in_=blk64[:, :]))
        L_(lambda e: e.dma_start(out=blk64Tt[:, :], in_=blk64T[:, :]))
        L_(lambda e: e.dma_start(out=bctt[:, :], in_=bct[:, :]))
        L_(lambda e: e.dma_start(out=mpret[:, :], in_=mpre[:, :]))
        for j in range(8):
            L_(lambda e, j=j: e.dma_start(out=xnf[:, j * TPC:(j + 1) * TPC],
                                          in_=xn_fm[j * 128:(j + 1) * 128, :]))
            L_(lambda e, j=j: e.dma_start(out=xpre_sb[:, j * 3:(j + 1) * 3],
                                          in_=xpre[j * 128:(j + 1) * 128, :]))

        for m in range(53):
            mw = 128 if m < 52 else 32
            par = m % 2
            for j in range(8):
                L_(lambda e, m=m, j=j, par=par, mw=mw: e.dma_start(
                    out=wctb[par][:, j * 128:j * 128 + mw],
                    in_=wct_b[j * 128:(j + 1) * 128, m * 128:m * 128 + mw]))
            for cc in range(4):
                for j in range(8):
                    M_(lambda e, j=j, cc=cc, par=par, mw=mw: e.matmul(
                        pp[cc % 2][0:mw, 0:512],
                        wctb[par][:, j * 128:j * 128 + mw],
                        xnf[:, j * TPC + cc * 512:j * TPC + (cc + 1) * 512],
                        start=(j == 0), stop=(j == 7)))
                V_(lambda e, cc=cc, m=m, mw=mw, par=par: e.tensor_scalar(
                    out=evb[par][0:mw, cc * 512:(cc + 1) * 512],
                    in0=pp[cc % 2][0:mw, 0:512], scalar1=bctt[0:mw, m:m + 1],
                    scalar2=None, op0=ALU.add))
            if 20 <= m <= 43:
                for j in range(8):
                    M_(lambda e, j=j, mw=mw: e.matmul(
                        pp[4][0:mw, 0:3], wctb[m % 2][:, j * 128:j * 128 + mw],
                        xpre_sb[:, j * 3:(j + 1) * 3],
                        start=(j == 0), stop=(j == 7)))
                V_(lambda e, m=m, mw=mw: e.tensor_scalar(
                    out=t3[0:mw, :], in0=pp[4][0:mw, 0:3],
                    scalar1=bctt[0:mw, m:m + 1], scalar2=None, op0=ALU.add))
                V_(lambda e, m=m, mw=mw: e.tensor_tensor(
                    out=qkvpre[0:mw, (m - 20) * 3:(m - 19) * 3], in0=t3[0:mw, :],
                    in1=mpret[0:mw, :], op=ALU.mult))
            S_(lambda e, m=m, mw=mw, par=par: e.dma_start(
                out=ctrl_scr[m * 128:m * 128 + mw, :], in_=evb[par][0:mw, :]))

        for pc in range(8):
            L_(lambda e, pc=pc: e.dma_start(out=dv0[:, :],
                                            in_=ctrl_scr[1536 + 128 * pc:1664 + 128 * pc, :]))
            K_(lambda e: e.activation(out=dv1[:, :], in_=dv0[:, :], func=AF.Silu))
            S_(lambda e, pc=pc: e.dma_start(out=csil[128 * pc:128 * (pc + 1), :],
                                            in_=dv1[:, :]))
        for parity, (udst, rdst) in enumerate([(u1s, rc1s), (u2s, rc2s)]):
            for pc in range(4):
                L_(lambda e, pc=pc, parity=parity: e.dma_start(
                    out=dv0[:, :], in_=ctrl_scr[256 * pc + parity:256 * (pc + 1):2, :]))
                K_(lambda e: e.activation(out=dv1[:, :], in_=dv0[:, :], func=AF.Sigmoid))
                S_(lambda e, pc=pc, udst=udst: e.dma_start(
                    out=udst[128 * pc:128 * (pc + 1), :], in_=dv1[:, :]))
                L_(lambda e, pc=pc, parity=parity: e.dma_start(
                    out=dv2[:, :], in_=csil[256 * pc + parity:256 * (pc + 1):2, :]))
                V_(lambda e: e.tensor_scalar(out=dv0[:, :], in0=dv1[:, :], scalar1=-1.0,
                                             scalar2=1.0, op0=ALU.mult, op1=ALU.add))
                V_(lambda e: e.tensor_tensor(out=dv0[:, :], in0=dv0[:, :], in1=dv2[:, :],
                                             op=ALU.mult))
                S_(lambda e, pc=pc, rdst=rdst: e.dma_start(
                    out=rdst[128 * pc:128 * (pc + 1), :], in_=dv0[:, :]))
        for pc in range(4):
            L_(lambda e, pc=pc: e.dma_start(out=dv0[:, :],
                                            in_=ctrl_scr[1024 + 128 * pc:1152 + 128 * pc, :]))
            for which, dst in (("s", sass), ("c", cass)):
                if which == "c":
                    V_(lambda e: e.tensor_scalar(out=dv0[:, :], in0=dv0[:, :],
                                                 scalar1=PI / 2, scalar2=None, op0=ALU.add))
                V_(lambda e: e.tensor_scalar(out=dv1[:, :], in0=dv0[:, :],
                                             scalar1=1.0 / (2 * PI), scalar2=MAGIC,
                                             op0=ALU.mult, op1=ALU.add))
                V_(lambda e: e.tensor_scalar(out=dv1[:, :], in0=dv1[:, :], scalar1=MAGIC,
                                             scalar2=None, op0=ALU.subtract))
                V_(lambda e: e.scalar_tensor_tensor(out=dv1[:, :], in0=dv1[:, :],
                                                    scalar=-2 * PI, in1=dv0[:, :],
                                                    op0=ALU.mult, op1=ALU.add))
                K_(lambda e: e.activation(out=dv2[:, :], in_=dv1[:, :], func=AF.Sin))
                S_(lambda e, pc=pc, dst=dst: e.dma_start(
                    out=dst[128 * pc:128 * (pc + 1), :], in_=dv2[:, :]))
        for pc in range(8):
            L_(lambda e, pc=pc: e.dma_start(out=dv0[:, :],
                                            in_=ctrl_scr[5664 + 128 * pc:5792 + 128 * pc, :]))
            K_(lambda e: e.activation(out=vstg[:, :], in_=dv0[:, :], func=AF.Sigmoid))
            S_(lambda e, pc=pc: e.dma_start(out=gate_s[128 * pc:128 * (pc + 1), :],
                                            in_=vstg[:, :]))
        L_(lambda e: e.dma_start(out=ab2[:, :], in_=ctrl_scr[5632:5648, :]))
        L_(lambda e: e.dma_start(out=be2[:, :], in_=ctrl_scr[5648:5664, :]))

        def logsig(dst, src):
            K_(lambda e: e.activation(out=dv2[0:16, :], in_=src[:, :], func=AF.Abs))
            K_(lambda e: e.activation(out=dv2[0:16, :], in_=dv2[0:16, :], func=AF.Exp,
                                      scale=-1.0))
            K_(lambda e: e.activation(out=dv2[0:16, :], in_=dv2[0:16, :], func=AF.Ln,
                                      bias=1.0))
            V_(lambda e: e.tensor_scalar(out=dst[:, :], in0=src[:, :], scalar1=0.0,
                                         scalar2=None, op0=ALU.min))
            V_(lambda e: e.tensor_tensor(out=dst[:, :], in0=dst[:, :], in1=dv2[0:16, :],
                                         op=ALU.subtract))

        logsig(la2, ab2)
        logsig(lb2, be2)
        K_(lambda e: e.activation(out=bex2[:, :], in_=be2[:, :], func=AF.Sigmoid))
        for ch in range(16):
            V_(lambda e, ch=ch: e.tensor_tensor_scan(
                out=L2[:, ch * 128:(ch + 1) * 128], data0=ones128t[0:16, 0:128],
                data1=la2[:, ch * 128:(ch + 1) * 128], initial=0.0,
                op0=ALU.mult, op1=ALU.add))
            V_(lambda e, ch=ch: e.tensor_scalar(
                out=nRC2[:, ch * 128:(ch + 1) * 128],
                in0=L2[:, ch * 128:(ch + 1) * 128],
                scalar1=L2[:, ch * 128 + 127:ch * 128 + 128], scalar2=None,
                op0=ALU.subtract))
            K_(lambda e, ch=ch: e.activation(out=eLc[:, ch:ch + 1],
                                             in_=L2[:, ch * 128 + 127:ch * 128 + 128],
                                             func=AF.Exp))
        V_(lambda e: e.tensor_tensor(out=lp2[:, :], in0=L2[:, :], in1=la2[:, :],
                                     op=ALU.subtract))
        V_(lambda e: e.tensor_tensor(out=lp2[:, :], in0=lp2[:, :], in1=lb2[:, :],
                                     op=ALU.add))
        for dst, srct in ((Lr16, L2), (nRC16, nRC2), (bex16, bex2), (lplb16, lp2)):
            S_(lambda e, dst=dst, srct=srct: e.dma_start(out=dst[:, :], in_=srct[:, :]))
        S_(lambda e: e.dma_start(out=eLc16[:, :], in_=eLc[:, :]))

        specs = (("q", qcw, 2560, 20, qfs, None), ("k", kcw, 3584, 28, kfs, ktms),
                 ("v", vcw, 4608, 36, None, vtms))
        for which, cw, base, mb, fdst, tdst in specs:
            for j in range(8):
                L_(lambda e, cw=cw, j=j: e.dma_start(out=cwt[:, :],
                                                     in_=cw[j * 128:(j + 1) * 128, :]))
                V_(lambda e, mb=mb, j=j: e.tensor_copy(
                    out=dpad[:, 0:3], in_=qkvpre[:, (mb - 20 + j) * 3:(mb - 19 + j) * 3]))
                L_(lambda e, base=base, j=j: e.dma_start(
                    out=dpad[:, 3:3 + TPC],
                    in_=ctrl_scr[base + j * 128:base + (j + 1) * 128, :]))
                V_(lambda e: e.tensor_scalar(out=dv0[:, :], in0=dpad[:, 0:TPC],
                                             scalar1=cwt[:, 0:1], scalar2=None,
                                             op0=ALU.mult))
                for tap in range(1, 4):
                    V_(lambda e, tap=tap: e.scalar_tensor_tensor(
                        out=dv0[:, :], in0=dpad[:, tap:tap + TPC],
                        scalar=cwt[:, tap:tap + 1], in1=dv0[:, :],
                        op0=ALU.mult, op1=ALU.add))
                K_(lambda e: e.activation(out=dv1[:, :], in_=dv0[:, :], func=AF.Silu))
                if which in ("q", "k"):
                    K_(lambda e: e.activation(out=dpad[:, 0:TPC], in_=dv1[:, :],
                                              func=AF.Square))
                    for cc in range(4):
                        M_(lambda e, cc=cc: e.matmul(
                            pp[6 + cc % 2][0:2, 0:512], blk64t[:, :],
                            dpad[:, cc * 512:(cc + 1) * 512], start=True, stop=True))
                        V_(lambda e, cc=cc: e.tensor_scalar(
                            out=dv2[0:2, cc * 512:(cc + 1) * 512],
                            in0=pp[6 + cc % 2][0:2, 0:512], scalar1=1e-12,
                            scalar2=None, op0=ALU.add))
                    K_(lambda e: e.activation(out=dv2[0:2, :], in_=dv2[0:2, :], func=AF.Ln))
                    K_(lambda e: e.activation(out=dv2[0:2, :], in_=dv2[0:2, :], func=AF.Exp,
                                              scale=-0.5))
                    for cc in range(4):
                        M_(lambda e, cc=cc: e.matmul(
                            pp[6 + cc % 2][0:128, 0:512], blk64Tt[:, :],
                            dv2[0:2, cc * 512:(cc + 1) * 512], start=True, stop=True))
                        V_(lambda e, cc=cc: e.tensor_tensor(
                            out=vstg[:, cc * 512:(cc + 1) * 512],
                            in0=dv1[:, cc * 512:(cc + 1) * 512],
                            in1=pp[6 + cc % 2][0:128, 0:512], op=ALU.mult))
                else:
                    K_(lambda e: e.copy(out=vstg[:, :], in_=dv1[:, :]))
                if fdst is not None:
                    S_(lambda e, fdst=fdst, j=j: e.dma_start(
                        out=fdst[j * 128:(j + 1) * 128, :], in_=vstg[:, :]))
                if tdst is not None:
                    for ti in range(16):
                        M_(lambda e, ti=ti: e.transpose(
                            pp[ti % 2].ap().bitcast(BF16)[:, 0:128],
                            vstg[:, ti * 128:(ti + 1) * 128], eyebt[:, :]))
                        V_(lambda e, ti=ti: e.tensor_copy(
                            out=tstg[ti % 2][:, :],
                            in_=pp[ti % 2].ap().bitcast(BF16)[:, 0:128]))
                        S_(lambda e, tdst=tdst, ti=ti, j=j: e.dma_start(
                            out=tdst[ti * 128:(ti + 1) * 128, j * 128:(j + 1) * 128],
                            in_=tstg[ti % 2][:, :]))

        s.wait("sync", ("dm", s.val("dm")))
        s.wait("gpsimd", ("gld", s.val("gld")))

    block = nc.Block()
    block.__enter__()
    sch.build(emit, block)
    block.__exit__(None, None, None)
    return nc


def build_attn1b(debug=False):
    nc = _nc()
    p = P(nc)
    qBd = nc.dram_tensor("qBd", [4 * 64, 8192], BF16, kind="ExternalInput").ap()
    kBd = nc.dram_tensor("kBd", [4 * 64, 8192], BF16, kind="ExternalInput").ap()
    ktmB = nc.dram_tensor("ktmB", [4 * 8192, 64], BF16, kind="ExternalInput").ap()
    vtmB = nc.dram_tensor("vtmB", [4 * 8192, 64], BF16, kind="ExternalInput").ap()
    LrB = nc.dram_tensor("LrB", [4, 8192], F32, kind="ExternalInput").ap()
    nRCB = nc.dram_tensor("nRCB", [4, 8192], F32, kind="ExternalInput").ap()
    bexB = nc.dram_tensor("bexB", [4, 8192], F32, kind="ExternalInput").ap()
    lplbB = nc.dram_tensor("lplbB", [4, 8192], F32, kind="ExternalInput").ap()
    eLcB = nc.dram_tensor("eLcB", [1, 256], F32, kind="ExternalInput").ap()
    rmw64 = nc.dram_tensor("rmw64", [128, 64], F32, kind="ExternalInput").ap()
    mfsB = nc.dram_tensor("mfsB", [256, 8192], BF16, kind="ExternalOutput").ap()
    kdbg = "ExternalOutput" if debug else "Internal"
    readsB = nc.dram_tensor("readsB", [4 * 8192, 64], F32, kind=kdbg).ap()

    ones128 = nc.inline_tensor(np.ones((128, 128), np.float32), name="ones128x128")
    eyef = nc.inline_tensor(np.eye(128, dtype=np.float32), name="eye128f")
    eyeb = nc.inline_tensor(np.eye(128, dtype=_bf16), name="eye128")
    ii, jj = np.indices((128, 128))
    nstri_np = np.where(jj > ii, -1.0, 0.0).astype(np.float32)
    utrid_np = np.where(jj >= ii, 1.0, 0.0).astype(np.float32)
    nstri_c = nc.inline_tensor(nstri_np, name="nstri")
    utrid_c = nc.inline_tensor(utrid_np, name="utrid")
    onesr = nc.inline_tensor(np.ones((1, 128), np.float32), name="ones1x128")

    ones128t = p.sb("ones128t", [128, 128])
    eyeft = p.sb("eyeft", [128, 128])
    eyebt = p.sb("eyebt", [128, 128], BF16)
    nstri = p.sb("nstri_t", [128, 128])
    utrid = p.sb("utrid_t", [128, 128])
    onrw = p.sb("onrw", [1, 128])
    rmwt = p.sb("rmwt", [128, 64])
    eLct = p.sb("eLct", [1, 256])
    rstk = p.sb("rstk", [4, 128])
    colz = p.sb("colz", [128, 12])
    gcol = p.sb("gcol", [128, 1])
    qhc = p.sb("qhc", [64, 128], BF16)
    khc = p.sb("khc", [64, 128], BF16)
    ktmc = p.sb("ktmc", [128, 64], BF16)
    vtmc = p.sb("vtmc", [128, 64], BF16)
    Ssb = p.sb("Ssb", [64, 64])
    Sbf = p.sb("Sbf", [64, 64], BF16)
    cxt = p.sb("cxt", [128, 128])
    ext = p.sb("ext", [128, 128])
    xT = p.sb("xT", [128, 128], BF16)
    xN = p.sb("xN", [128, 128], BF16)
    xT2 = p.sb("xT2", [128, 128], BF16)
    xN2 = p.sb("xN2", [128, 128], BF16)
    mT = p.sb("mT", [128, 128], BF16)
    usb = p.sb("usb", [128, 64])
    ubf = p.sb("ubf", [128, 64], BF16)
    ktl = p.sb("ktl", [128, 64], BF16)
    rdt = p.sb("rdt", [128, 64])
    sq64 = p.sb("sq64", [128, 64])
    nr1 = p.sb("nr1", [128, 1])
    mb = p.sb("mb", [128, 64], BF16)
    tst = p.sb("tst", [64, 128], BF16)
    pp = [p.ps(f"pp{i}", [128, 512]) for i in range(6)]

    sch = Sched(p)

    def emit(s):
        V_, K_, M_, L_, S_, G_ = _mkW(s)
        for dst, src in ((ones128t, ones128), (eyeft, eyef), (eyebt, eyeb),
                         (nstri, nstri_c), (utrid, utrid_c), (onrw, onesr),
                         (rmwt, rmw64), (eLct, eLcB)):
            L_(lambda e, dst=dst, src=src: e.dma_start(out=dst[:, :], in_=src[:, :]))
        for sc in range(4):
            V_(lambda e: e.memset(Ssb[:, :], 0.0))
            for ch in range(64):
                t0 = sc * 8192 + ch * 128
                c0 = ch * 128
                for r, src in enumerate((LrB, nRCB, bexB, lplbB)):
                    L_(lambda e, r=r, src=src, sc=sc, c0=c0: e.dma_start(
                        out=rstk[r:r + 1, :], in_=src[sc:sc + 1, c0:c0 + 128]))
                M_(lambda e: e.transpose(pp[1][0:128, 0:4], rstk[:, :], eyeft[0:4, 0:4]))
                V_(lambda e: e.tensor_copy(out=colz[:, 0:4], in_=pp[1][0:128, 0:4]))
                K_(lambda e: e.activation(out=colz[:, 4:5], in_=colz[:, 0:1], func=AF.Exp))
                K_(lambda e: e.activation(out=colz[:, 5:6], in_=colz[:, 1:2], func=AF.Exp,
                                          scale=-1.0))
                K_(lambda e: e.activation(out=colz[:, 6:7], in_=colz[:, 3:4], func=AF.Exp))
                V_(lambda e: e.tensor_scalar(out=colz[:, 7:8], in0=colz[:, 0:1],
                                             scalar1=-1.0, scalar2=None, op0=ALU.mult))
                V_(lambda e: e.tensor_scalar(out=colz[:, 8:9], in0=colz[:, 6:7],
                                             scalar1=-1.0, scalar2=None, op0=ALU.mult))
                M_(lambda e, sc=sc, ch=ch: e.matmul(
                    pp[5][0:128, 0:1], onrw[:, :], eLct[0:1, sc * 64 + ch:sc * 64 + ch + 1],
                    start=True, stop=True))
                V_(lambda e: e.tensor_copy(out=gcol[:, :], in_=pp[5][0:128, 0:1]))
                L_(lambda e, sc=sc, c0=c0: e.dma_start(
                    out=qhc[:, :], in_=qBd[sc * 64:(sc + 1) * 64, c0:c0 + 128]))
                L_(lambda e, sc=sc, c0=c0: e.dma_start(
                    out=khc[:, :], in_=kBd[sc * 64:(sc + 1) * 64, c0:c0 + 128]))
                L_(lambda e, t0=t0: e.dma_start(out=ktmc[:, :], in_=ktmB[t0:t0 + 128, :]))
                L_(lambda e, t0=t0: e.dma_start(out=vtmc[:, :], in_=vtmB[t0:t0 + 128, :]))
                K_(lambda e: e.copy(out=Sbf[:, :], in_=Ssb[:, :]))
                M_(lambda e: e.matmul(pp[0][0:128, 0:128], khc[:, :], khc[:, :],
                                      start=True, stop=True))
                V_(lambda e: e.tensor_scalar(out=cxt[:, :], in0=ones128t[:, :],
                                             scalar1=colz[:, 3:4], scalar2=None,
                                             op0=ALU.mult))
                M_(lambda e: e.transpose(pp[1][0:128, 0:128], cxt[:, :], eyeft[:, :]))
                V_(lambda e: e.tensor_scalar(out=ext[:, :], in0=pp[1][0:128, 0:128],
                                             scalar1=colz[:, 7:8], scalar2=0.0,
                                             op0=ALU.add, op1=ALU.min))
                K_(lambda e: e.activation(out=ext[:, :], in_=ext[:, :], func=AF.Exp))
                V_(lambda e: e.tensor_tensor(out=ext[:, :], in0=ext[:, :],
                                             in1=pp[0][0:128, 0:128], op=ALU.mult))
                V_(lambda e: e.tensor_tensor(out=ext[:, :], in0=ext[:, :], in1=nstri[:, :],
                                             op=ALU.mult))
                K_(lambda e: e.copy(out=xT[:, :], in_=ext[:, :]))
                M_(lambda e: e.transpose(pp[1][0:128, 0:128], ext[:, :], eyeft[:, :]))
                K_(lambda e: e.copy(out=xN[:, :], in_=pp[1][0:128, 0:128]))
                M_(lambda e: e.matmul(pp[2][0:128, 0:64], khc[:, :], Sbf[:, :],
                                      start=True, stop=True))
                V_(lambda e: e.tensor_scalar(out=usb[:, :], in0=vtmc[:, :],
                                             scalar1=colz[:, 2:3], scalar2=None,
                                             op0=ALU.mult))
                V_(lambda e: e.scalar_tensor_tensor(out=usb[:, :], in0=pp[2][0:128, 0:64],
                                                    scalar=colz[:, 8:9], in1=usb[:, :],
                                                    op0=ALU.mult, op1=ALU.add))
                K_(lambda e: e.copy(out=ubf[:, :], in_=usb[:, :]))
                cT, cN, oT, oN = xT, xN, xT2, xN2
                for r in range(7):
                    M_(lambda e, cT=cT: e.matmul(pp[2][0:128, 0:64], cT[:, :], ubf[:, :],
                                                 start=True, stop=True))
                    V_(lambda e: e.tensor_tensor(out=usb[:, :], in0=usb[:, :],
                                                 in1=pp[2][0:128, 0:64], op=ALU.add))
                    K_(lambda e: e.copy(out=ubf[:, :], in_=usb[:, :]))
                    if r < 6:
                        M_(lambda e, cT=cT, cN=cN: e.matmul(
                            pp[3][0:128, 0:128], cT[:, :], cN[:, :], start=True, stop=True))
                        K_(lambda e, oN=oN: e.copy(out=oN[:, :], in_=pp[3][0:128, 0:128]))
                        M_(lambda e, oN=oN: e.transpose(
                            pp[4].ap().bitcast(BF16)[:, 0:128], oN[:, :], eyebt[:, :]))
                        V_(lambda e, oT=oT: e.tensor_copy(
                            out=oT[:, :], in_=pp[4].ap().bitcast(BF16)[:, 0:128]))
                        cT, cN, oT, oN = oT, oN, cT, cN
                M_(lambda e: e.matmul(pp[0][0:128, 0:128], khc[:, :], qhc[:, :],
                                      start=True, stop=True))
                V_(lambda e: e.tensor_scalar(out=cxt[:, :], in0=ones128t[:, :],
                                             scalar1=colz[:, 0:1], scalar2=None,
                                             op0=ALU.mult))
                M_(lambda e: e.transpose(pp[1][0:128, 0:128], cxt[:, :], eyeft[:, :]))
                V_(lambda e: e.tensor_scalar(out=ext[:, :], in0=pp[1][0:128, 0:128],
                                             scalar1=colz[:, 7:8], scalar2=0.0,
                                             op0=ALU.add, op1=ALU.min))
                K_(lambda e: e.activation(out=ext[:, :], in_=ext[:, :], func=AF.Exp))
                V_(lambda e: e.tensor_tensor(out=ext[:, :], in0=ext[:, :],
                                             in1=pp[0][0:128, 0:128], op=ALU.mult))
                V_(lambda e: e.tensor_tensor(out=ext[:, :], in0=ext[:, :], in1=utrid[:, :],
                                             op=ALU.mult))
                K_(lambda e: e.copy(out=mT[:, :], in_=ext[:, :]))
                M_(lambda e: e.matmul(pp[2][0:128, 0:64], qhc[:, :], Sbf[:, :],
                                      start=True, stop=True))
                V_(lambda e: e.tensor_scalar(out=rdt[:, :], in0=pp[2][0:128, 0:64],
                                             scalar1=colz[:, 4:5], scalar2=None,
                                             op0=ALU.mult))
                M_(lambda e: e.matmul(pp[2][0:128, 0:64], mT[:, :], ubf[:, :],
                                      start=True, stop=True))
                V_(lambda e: e.tensor_tensor(out=rdt[:, :], in0=rdt[:, :],
                                             in1=pp[2][0:128, 0:64], op=ALU.add))
                V_(lambda e: e.tensor_scalar(out=ktl[:, :], in0=ktmc[:, :],
                                             scalar1=colz[:, 5:6], scalar2=None,
                                             op0=ALU.mult))
                M_(lambda e: e.matmul(pp[2][0:64, 0:64], ktl[:, :], ubf[:, :],
                                      start=True, stop=True))
                V_(lambda e: e.scalar_tensor_tensor(out=Ssb[:, :], in0=Ssb[:, :],
                                                    scalar=gcol[0:64, :],
                                                    in1=pp[2][0:64, 0:64],
                                                    op0=ALU.mult, op1=ALU.add))
                if debug:
                    S_(lambda e, t0=t0: e.dma_start(out=readsB[t0:t0 + 128, :],
                                                    in_=rdt[:, :]))
                K_(lambda e: e.activation(out=sq64[:, :], in_=rdt[:, :], func=AF.Square))
                V_(lambda e: e.reduce_sum(out=nr1[:, :], in_=sq64[:, :], axis=AX.X,
                                          negate=False))
                V_(lambda e: e.tensor_scalar(out=nr1[:, :], in0=nr1[:, :],
                                             scalar1=1.0 / 64, scalar2=1e-6,
                                             op0=ALU.mult, op1=ALU.add))
                K_(lambda e: e.activation(out=nr1[:, :], in_=nr1[:, :], func=AF.Ln))
                K_(lambda e: e.activation(out=nr1[:, :], in_=nr1[:, :], func=AF.Exp,
                                          scale=-0.5))
                V_(lambda e: e.tensor_scalar(out=rdt[:, :], in0=rdt[:, :],
                                             scalar1=nr1[:, 0:1], scalar2=None,
                                             op0=ALU.mult))
                V_(lambda e: e.tensor_tensor(out=rdt[:, :], in0=rdt[:, :], in1=rmwt[:, :],
                                             op=ALU.mult))
                K_(lambda e: e.copy(out=mb[:, :], in_=rdt[:, :]))
                M_(lambda e: e.transpose(pp[4].ap().bitcast(BF16)[0:64, 0:128],
                                         mb[:, :], eyebt[:, :]))
                V_(lambda e: e.tensor_copy(out=tst[:, :],
                                           in_=pp[4].ap().bitcast(BF16)[0:64, 0:128]))
                S_(lambda e, sc=sc, c0=c0: e.dma_start(
                    out=mfsB[sc * 64:(sc + 1) * 64, c0:c0 + 128], in_=tst[:, :]))

        s.wait("sync", ("dm", s.val("dm")))

    block = nc.Block()
    block.__enter__()
    sch.build(emit, block)
    block.__exit__(None, None, None)
    return nc


def build_attn1c():
    nc = _nc()
    p = P(nc)
    srcs = {}
    for nm in ("u1B", "u2B", "rc1B", "rc2B", "caB", "saB"):
        srcs[nm] = nc.dram_tensor(nm, [128, 8192], F32, kind="ExternalInput").ap()
    wbcB = nc.dram_tensor("wbcB", [128, 4], F32, kind="ExternalInput").ap()
    vf1B = nc.dram_tensor("vf1B", [128, 8192], BF16, kind="ExternalOutput").ap()
    vf2B = nc.dram_tensor("vf2B", [128, 8192], BF16, kind="ExternalOutput").ap()

    wbt = p.sb("wbt", [128, 4])
    vu1 = p.sb("vu1", [128, 4096])
    vu2 = p.sb("vu2", [128, 4096])
    vr1 = p.sb("vr1", [128, 4096])
    vr2 = p.sb("vr2", [128, 4096])
    vca = p.sb("vca", [128, 4096])
    vsa = p.sb("vsa", [128, 4096])
    SV1 = p.sb("SV1", [128, 96])
    SV2 = p.sb("SV2", [128, 96])
    w1v = p.sb("w1v", [128, 96])
    w2v = p.sb("w2v", [128, 96])
    r1t = p.sb("r1t", [128, 96])
    r2t = p.sb("r2t", [128, 96])
    tmt = p.sb("tmt", [128, 96])
    fin1 = p.sb("fin1", [128, 192])
    fin2 = p.sb("fin2", [128, 192])
    vin1 = p.sb("vin1", [128, 65])
    vin2 = p.sb("vin2", [128, 65])
    vo1 = p.sb("vo1", [128, 2])
    vo2 = p.sb("vo2", [128, 2])
    vfmb1 = p.sb("vfmb1", [128, 4096], BF16)
    vfmb2 = p.sb("vfmb2", [128, 4096], BF16)

    sch = Sched(p)
    arrs = (vu1, vu2, vr1, vr2, vca, vsa)
    nms = ("u1B", "u2B", "rc1B", "rc2B", "caB", "saB")

    def emit(s):
        V_, K_, M_, L_, S_, G_ = _mkW(s)
        L_(lambda e: e.dma_start(out=wbt[:, :], in_=wbcB[:, :]))

        def loadr(r):
            for dst, nm in zip(arrs, nms):
                L_(lambda e, dst=dst, nm=nm, r=r: e.dma_start(
                    out=dst[:, :], in_=srcs[nm][:, r * 4096:(r + 1) * 4096]))

        def stepmats(j, SA, SB, width):
            V_(lambda e, SA=SA, width=width: e.tensor_scalar(
                out=w1v[:, 0:width], in0=SA, scalar1=wbt[:, 0:1], scalar2=None,
                op0=ALU.mult))
            V_(lambda e, SB=SB, width=width: e.scalar_tensor_tensor(
                out=w1v[:, 0:width], in0=SB, scalar=wbt[:, 2:3], in1=w1v[:, 0:width],
                op0=ALU.mult, op1=ALU.add))
            V_(lambda e, SA=SA, width=width: e.tensor_scalar(
                out=w2v[:, 0:width], in0=SA, scalar1=wbt[:, 1:2], scalar2=None,
                op0=ALU.mult))
            V_(lambda e, SB=SB, width=width: e.scalar_tensor_tensor(
                out=w2v[:, 0:width], in0=SB, scalar=wbt[:, 3:4], in1=w2v[:, 0:width],
                op0=ALU.mult, op1=ALU.add))

        for r in range(2):
            loadr(r)
            for tile, vals in ((SV1, (0.0, 1.0, 0.0)), (SV2, (0.0, 0.0, 1.0))):
                for v in range(3):
                    V_(lambda e, tile=tile, v=v, vals=vals: e.memset(
                        tile[:, v * 32:(v + 1) * 32], vals[v]))
            for j in range(128):
                js = slice(j * 32, (j + 1) * 32)
                stepmats(j, SV1[:, :], SV2[:, :], 96)
                for v in range(3):
                    vs = slice(v * 32, (v + 1) * 32)
                    V_(lambda e, vs=vs, js=js: e.tensor_tensor(
                        out=r1t[:, vs], in0=w1v[:, vs], in1=vca[:, js], op=ALU.mult))
                    V_(lambda e, vs=vs, js=js: e.tensor_tensor(
                        out=tmt[:, vs], in0=w2v[:, vs], in1=vsa[:, js], op=ALU.mult))
                    V_(lambda e, vs=vs: e.tensor_tensor(
                        out=r1t[:, vs], in0=r1t[:, vs], in1=tmt[:, vs], op=ALU.subtract))
                    V_(lambda e, vs=vs, js=js: e.tensor_tensor(
                        out=r2t[:, vs], in0=w1v[:, vs], in1=vsa[:, js], op=ALU.mult))
                    V_(lambda e, vs=vs, js=js: e.tensor_tensor(
                        out=tmt[:, vs], in0=w2v[:, vs], in1=vca[:, js], op=ALU.mult))
                    V_(lambda e, vs=vs: e.tensor_tensor(
                        out=r2t[:, vs], in0=r2t[:, vs], in1=tmt[:, vs], op=ALU.add))
                    V_(lambda e, vs=vs, js=js: e.tensor_tensor(
                        out=SV1[:, vs], in0=vu1[:, js], in1=r1t[:, vs], op=ALU.mult))
                    V_(lambda e, vs=vs, js=js: e.tensor_tensor(
                        out=SV2[:, vs], in0=vu2[:, js], in1=r2t[:, vs], op=ALU.mult))
                V_(lambda e, js=js: e.tensor_tensor(
                    out=SV1[:, 0:32], in0=SV1[:, 0:32], in1=vr1[:, js], op=ALU.add))
                V_(lambda e, js=js: e.tensor_tensor(
                    out=SV2[:, 0:32], in0=SV2[:, 0:32], in1=vr2[:, js], op=ALU.add))
            V_(lambda e, r=r: e.tensor_copy(out=fin1[:, r * 96:(r + 1) * 96],
                                            in_=SV1[:, :]))
            V_(lambda e, r=r: e.tensor_copy(out=fin2[:, r * 96:(r + 1) * 96],
                                            in_=SV2[:, :]))
        V_(lambda e: e.memset(vin1[:, 0:1], 0.0))
        V_(lambda e: e.memset(vin2[:, 0:1], 0.0))
        for ch in range(64):
            r, cc = ch // 32, ch % 32
            F1 = fin1[:, r * 96 + cc:r * 96 + cc + 1]
            p11 = fin1[:, r * 96 + 32 + cc:r * 96 + 32 + cc + 1]
            p12 = fin1[:, r * 96 + 64 + cc:r * 96 + 64 + cc + 1]
            F2 = fin2[:, r * 96 + cc:r * 96 + cc + 1]
            p21 = fin2[:, r * 96 + 32 + cc:r * 96 + 32 + cc + 1]
            p22 = fin2[:, r * 96 + 64 + cc:r * 96 + 64 + cc + 1]
            vi1 = vin1[:, ch:ch + 1]
            vi2 = vin2[:, ch:ch + 1]
            V_(lambda e, p11=p11, vi1=vi1: e.tensor_tensor(
                out=vo1[:, 0:1], in0=p11, in1=vi1, op=ALU.mult))
            V_(lambda e, p12=p12, vi2=vi2: e.tensor_tensor(
                out=vo1[:, 1:2], in0=p12, in1=vi2, op=ALU.mult))
            V_(lambda e: e.tensor_tensor(out=vo1[:, 0:1], in0=vo1[:, 0:1],
                                         in1=vo1[:, 1:2], op=ALU.add))
            V_(lambda e, F1=F1, ch=ch: e.tensor_tensor(
                out=vin1[:, ch + 1:ch + 2], in0=vo1[:, 0:1], in1=F1, op=ALU.add))
            V_(lambda e, p21=p21, vi1=vi1: e.tensor_tensor(
                out=vo2[:, 0:1], in0=p21, in1=vi1, op=ALU.mult))
            V_(lambda e, p22=p22, vi2=vi2: e.tensor_tensor(
                out=vo2[:, 1:2], in0=p22, in1=vi2, op=ALU.mult))
            V_(lambda e: e.tensor_tensor(out=vo2[:, 0:1], in0=vo2[:, 0:1],
                                         in1=vo2[:, 1:2], op=ALU.add))
            V_(lambda e, F2=F2, ch=ch: e.tensor_tensor(
                out=vin2[:, ch + 1:ch + 2], in0=vo2[:, 0:1], in1=F2, op=ALU.add))
        for r in range(2):
            loadr(r)
            V_(lambda e, r=r: e.tensor_copy(out=SV1[:, 0:32],
                                            in_=vin1[:, r * 32:r * 32 + 32]))
            V_(lambda e, r=r: e.tensor_copy(out=SV2[:, 0:32],
                                            in_=vin2[:, r * 32:r * 32 + 32]))
            for j in range(128):
                js = slice(j * 32, (j + 1) * 32)
                stepmats(j, SV1[:, 0:32], SV2[:, 0:32], 32)
                V_(lambda e, js=js: e.tensor_tensor(
                    out=r1t[:, 0:32], in0=w1v[:, 0:32], in1=vca[:, js], op=ALU.mult))
                V_(lambda e, js=js: e.tensor_tensor(
                    out=tmt[:, 0:32], in0=w2v[:, 0:32], in1=vsa[:, js], op=ALU.mult))
                V_(lambda e: e.tensor_tensor(
                    out=r1t[:, 0:32], in0=r1t[:, 0:32], in1=tmt[:, 0:32],
                    op=ALU.subtract))
                V_(lambda e, js=js: e.tensor_tensor(
                    out=r2t[:, 0:32], in0=w1v[:, 0:32], in1=vsa[:, js], op=ALU.mult))
                V_(lambda e, js=js: e.tensor_tensor(
                    out=tmt[:, 0:32], in0=w2v[:, 0:32], in1=vca[:, js], op=ALU.mult))
                V_(lambda e: e.tensor_tensor(
                    out=r2t[:, 0:32], in0=r2t[:, 0:32], in1=tmt[:, 0:32], op=ALU.add))
                V_(lambda e, js=js: e.tensor_tensor(
                    out=SV1[:, 0:32], in0=vu1[:, js], in1=r1t[:, 0:32], op=ALU.mult))
                V_(lambda e, js=js: e.tensor_tensor(
                    out=SV1[:, 0:32], in0=SV1[:, 0:32], in1=vr1[:, js], op=ALU.add))
                V_(lambda e, js=js: e.tensor_tensor(
                    out=SV2[:, 0:32], in0=vu2[:, js], in1=r2t[:, 0:32], op=ALU.mult))
                V_(lambda e, js=js: e.tensor_tensor(
                    out=SV2[:, 0:32], in0=SV2[:, 0:32], in1=vr2[:, js], op=ALU.add))
                K_(lambda e, js=js: e.copy(out=vfmb1[:, js], in_=SV1[:, 0:32]))
                K_(lambda e, js=js: e.copy(out=vfmb2[:, js], in_=SV2[:, 0:32]))
            S_(lambda e, r=r: e.dma_start(out=vf1B[:, r * 4096:(r + 1) * 4096],
                                          in_=vfmb1[:, :]))
            S_(lambda e, r=r: e.dma_start(out=vf2B[:, r * 4096:(r + 1) * 4096],
                                          in_=vfmb2[:, :]))

        s.wait("sync", ("dm", s.val("dm")))

    block = nc.Block()
    block.__enter__()
    sch.build(emit, block)
    block.__exit__(None, None, None)
    return nc


def build_attn1d():
    nc = _nc()
    p = P(nc)
    mfm_d = nc.dram_tensor("mfm_d", [D, TPC], BF16, kind="ExternalInput").ap()
    vf1_d = nc.dram_tensor("vf1_d", [512, TPC], BF16, kind="ExternalInput").ap()
    vf2_d = nc.dram_tensor("vf2_d", [512, TPC], BF16, kind="ExternalInput").ap()
    gate_d = nc.dram_tensor("gate_d", [D, TPC], BF16, kind="ExternalInput").ap()
    wout_bd = nc.dram_tensor("wout_b", [2 * D, D], BF16, kind="ExternalInput").ap()
    rmsv_e = nc.dram_tensor("rmsv_e", [128, 4], F32, kind="ExternalInput").ap()
    rmsv_o = nc.dram_tensor("rmsv_o", [128, 4], F32, kind="ExternalInput").ap()
    boutd = nc.dram_tensor("boutt", [128, 8], F32, kind="ExternalInput").ap()
    attn_fm = nc.dram_tensor("attn_fm", [D, TPC], F32, kind="ExternalOutput").ap()

    onesr = nc.inline_tensor(np.ones((1, 128), np.float32), name="ones1x128")
    onesc = nc.inline_tensor(np.ones((128, 1), np.float32), name="ones128x1")
    onrw = p.sb("onrw", [1, 128])
    oncl = p.sb("oncl", [128, 1])
    rmve = p.sb("rmve", [128, 4])
    rmvo = p.sb("rmvo", [128, 4])
    boutt = p.sb("boutt_s", [128, 8])
    rhsM = p.sb("rhsM", [128, 8 * 512], BF16)
    rhsV = p.sb("rhsV", [128, 8 * 512], BF16)
    sqt = p.sb("sqt", [128, 512])
    rstd1 = p.sb("rstd1", [1, 512])
    woutb = [p.sb(f"woutb{i}", [128, 16 * 128], BF16) for i in range(2)]
    ggt = p.sb("ggt", [128, 512], BF16)
    ofm = [p.sb(f"ofm{i}", [128, 512]) for i in range(2)]
    pp = [p.ps(f"pp{i}", [128, 512]) for i in range(7)]

    sch = Sched(p)

    def emit(s):
        V_, K_, M_, L_, S_, G_ = _mkW(s)
        for dst, src in ((onrw, onesr), (oncl, onesc), (rmve, rmsv_e), (rmvo, rmsv_o),
                         (boutt, boutd)):
            L_(lambda e, dst=dst, src=src: e.dma_start(out=dst[:, :], in_=src[:, :]))
        for tc in range(4):
            cs = slice(tc * 512, (tc + 1) * 512)
            for kj in range(8):
                L_(lambda e, kj=kj, cs=cs: e.dma_start(
                    out=rhsM[:, kj * 512:(kj + 1) * 512],
                    in_=mfm_d[kj * 128:(kj + 1) * 128, cs]))
            for kv in range(8):
                src = vf1_d if kv < 4 else vf2_d
                r0 = (kv % 4) * 128
                L_(lambda e, kv=kv, src=src, r0=r0, cs=cs: e.dma_start(
                    out=rhsV[:, kv * 512:(kv + 1) * 512], in_=src[r0:r0 + 128, cs]))
            for kv in range(8):
                K_(lambda e, kv=kv: e.activation(out=sqt[:, :],
                                                 in_=rhsV[:, kv * 512:(kv + 1) * 512],
                                                 func=AF.Square))
                M_(lambda e, kv=kv: e.matmul(pp[4][0:1, 0:512], oncl[:, :], sqt[:, :],
                                             start=(kv == 0), stop=(kv == 7)))
            V_(lambda e: e.tensor_scalar(out=rstd1[:, :], in0=pp[4][0:1, 0:512],
                                         scalar1=1.0 / 1024, scalar2=1e-6,
                                         op0=ALU.mult, op1=ALU.add))
            K_(lambda e: e.activation(out=rstd1[:, :], in_=rstd1[:, :], func=AF.Ln))
            K_(lambda e: e.activation(out=rstd1[:, :], in_=rstd1[:, :], func=AF.Exp,
                                      scale=-0.5))
            M_(lambda e: e.matmul(pp[5][0:128, 0:512], onrw[:, :], rstd1[:, :],
                                  start=True, stop=True))
            for kv in range(8):
                vs = slice(kv * 512, (kv + 1) * 512)
                V_(lambda e, vs=vs: e.tensor_tensor(out=rhsV[:, vs], in0=rhsV[:, vs],
                                                    in1=pp[5][0:128, 0:512], op=ALU.mult))
                rc = rmve[:, kv:kv + 1] if kv < 4 else rmvo[:, kv - 4:kv - 3]
                V_(lambda e, vs=vs, rc=rc: e.tensor_scalar(
                    out=rhsV[:, vs], in0=rhsV[:, vs], scalar1=rc, scalar2=None,
                    op0=ALU.mult))
            for jo in range(8):
                for ki in range(16):
                    L_(lambda e, jo=jo, ki=ki: e.dma_start(
                        out=woutb[jo % 2][:, ki * 128:(ki + 1) * 128],
                        in_=wout_bd[ki * 128:(ki + 1) * 128, jo * 128:(jo + 1) * 128]))
                for ki in range(16):
                    rhs = rhsM if ki < 8 else rhsV
                    M_(lambda e, jo=jo, ki=ki, rhs=rhs: e.matmul(
                        pp[jo % 2][0:128, 0:512],
                        woutb[jo % 2][:, ki * 128:(ki + 1) * 128],
                        rhs[:, (ki % 8) * 512:(ki % 8 + 1) * 512],
                        start=(ki == 0), stop=(ki == 15)))
                V_(lambda e, jo=jo: e.tensor_scalar(
                    out=ofm[jo % 2][:, :], in0=pp[jo % 2][0:128, 0:512],
                    scalar1=boutt[:, jo:jo + 1], scalar2=None, op0=ALU.add))
                L_(lambda e, jo=jo, cs=cs: e.dma_start(
                    out=ggt[:, :], in_=gate_d[jo * 128:(jo + 1) * 128, cs]))
                V_(lambda e, jo=jo: e.tensor_tensor(out=ofm[jo % 2][:, :],
                                                    in0=ofm[jo % 2][:, :], in1=ggt[:, :],
                                                    op=ALU.mult))
                S_(lambda e, jo=jo, cs=cs: e.dma_start(
                    out=attn_fm[jo * 128:(jo + 1) * 128, cs], in_=ofm[jo % 2][:, :]))

        s.wait("sync", ("dm", s.val("dm")))

    block = nc.Block()
    block.__enter__()
    sch.build(emit, block)
    block.__exit__(None, None, None)
    return nc



def qpos(c):
    pos = []
    for qs in range(16):
        jq, sub = qs // 4, qs % 4
        t0 = jq * 2048 + (c % 4) * 512 + sub * 128
        pos.extend(range(t0, t0 + 128))
    return np.array(pos)


def _cc(a):
    return np.ascontiguousarray(a)


def _run(nc, maps):
    r = run_bass_kernel_spmd(nc, maps, core_ids=list(range(NCORE)))
    return [dict(res) for res in r.results]


def kernel(**inputs):
    d = inputs
    x = np.ascontiguousarray(np.asarray(d["L0_attn_x"], np.float32).reshape(2 * T, D))

    r_ln = _run(build_ln(), [{"x": _cc(x[c * TPC:(c + 1) * TPC])} for c in range(NCORE)])

    r_cw = _run(build_castw(), [{"w1": np.asarray(d["L0_ffn_W1"], np.float32),
                                 "w2": np.asarray(d["L0_ffn_W2"], np.float32)}] * NCORE)
    w1b, w2b = np.asarray(r_cw[0]["w1b"]), np.asarray(r_cw[0]["w2b"])

    wq = _cc(np.asarray(d["L0_attn_Wq"], np.float32).T)
    wk = _cc(np.asarray(d["L0_attn_Wk"], np.float32).T)
    wv = _cc(np.asarray(d["L0_attn_Wv"], np.float32).T)
    wd = _cc(np.asarray(d["L0_attn_Wd"], np.float32).T)
    bdc = _cc(np.broadcast_to(np.asarray(d["L0_attn_bd"], np.float32), (128, 1)))
    osc = _cc(np.broadcast_to(np.asarray(d["L0_attn_out_scale"], np.float32), (128, 1)))
    r_qk = _run(build_qkvc(), [{"xn_fm32": np.asarray(r_ln[c]["xn_fm32"]),
                                "wq": wq, "wk": wk, "wv": wv, "wd": wd,
                                "bdc": bdc, "oscol": osc} for c in range(NCORE)])

    wo = _cc(np.asarray(d["L0_attn_Wo"], np.float32).T)
    maps = []
    for c in range(NCORE):
        b = c // 4
        pos = qpos(c)
        qfull = np.concatenate([np.asarray(r_qk[4 * b + i]["q_fm"]) for i in range(4)],
                               axis=1)
        kfull = np.concatenate([np.asarray(r_qk[4 * b + i]["k_fm"]) for i in range(4)],
                               axis=1)
        vfull = np.concatenate([np.asarray(r_qk[4 * b + i]["v_tm"]) for i in range(4)],
                               axis=0)
        cfull = np.concatenate([np.asarray(r_qk[4 * b + i]["ccol"]) for i in range(4)],
                               axis=1)
        maps.append({"qf": _cc(qfull[:, pos]), "trow": _cc(pos[None, :].astype(np.float32)),
                     "kf": _cc(kfull), "vt": _cc(vfull), "ccol": _cc(cfull), "wo": wo})
    r_a0 = _run(build_attn0(), maps)
    a0_tm = np.zeros((2 * T, D), np.float32)
    for c in range(NCORE):
        b = c // 4
        a0_tm[b * T + qpos(c)] = np.asarray(r_a0[c]["attn_fm"]).T

    routers = np.asarray(d["L0_ffn_routers"], np.float32)
    r_f0 = _run(build_ffn0(), [{"xn_tm": np.asarray(r_ln[c]["xn_tm"]),
                                "routers": routers, "w1b": w1b, "w2b": w2b}
                               for c in range(NCORE)])

    r_rl = _run(build_ln(residual=True),
                [{"x": _cc(x[c * TPC:(c + 1) * TPC]),
                  "a_tm": _cc(a0_tm[c * TPC:(c + 1) * TPC]),
                  "f_tm": np.asarray(r_f0[c]["ffn_tm"])} for c in range(NCORE)])

    wct = _cc(np.asarray(d["L1_attn_W_ctrl"], np.float32))
    bctf = np.zeros(6784, np.float32)
    bctf[:CTRL] = np.asarray(d["L1_attn_b_ctrl"], np.float32)
    bct = _cc(bctf.reshape(53, 128).T)
    widx = np.concatenate([np.arange(D), D + 2 * np.arange(D // 2),
                           D + 1 + 2 * np.arange(D // 2)])
    wout_p = _cc(np.asarray(d["L1_attn_W_out"], np.float32)[widx])
    maps = []
    for c in range(NCORE):
        if c % 4 == 0:
            xpre = np.zeros((D, 3), ml_dtypes.bfloat16)
            mpre = np.zeros((128, 3), np.float32)
        else:
            xpre = _cc(np.asarray(r_rl[c - 1]["xn_fm"])[:, TPC - 3:TPC])
            mpre = np.ones((128, 3), np.float32)
        maps.append({"xn_fm": np.asarray(r_rl[c]["xn_fm"]), "xpre": xpre, "mpre": mpre,
                     "wct": wct, "bct": bct,
                     "qcw": _cc(np.asarray(d["L1_attn_qconv_w"], np.float32)),
                     "kcw": _cc(np.asarray(d["L1_attn_kconv_w"], np.float32)),
                     "vcw": _cc(np.asarray(d["L1_attn_vconv_w"], np.float32)),
                     "wout": wout_p})
    r_1a = _run(build_attn1a(), maps)

    rmw64 = _cc(np.broadcast_to(np.asarray(d["L1_attn_rms_m_w"], np.float32), (128, 64)))
    maps = []
    for c in range(NCORE):
        qB = np.zeros((256, T), ml_dtypes.bfloat16)
        kB = np.zeros((256, T), ml_dtypes.bfloat16)
        ktm = np.zeros((4 * T, 64), ml_dtypes.bfloat16)
        vtm = np.zeros((4 * T, 64), ml_dtypes.bfloat16)
        rows = {"LrB": ("Lr16", np.zeros((4, T), np.float32)),
                "nRCB": ("nRC16", np.zeros((4, T), np.float32)),
                "bexB": ("bex16", np.zeros((4, T), np.float32)),
                "lplbB": ("lplb16", np.zeros((4, T), np.float32))}
        eL = np.zeros((1, 256), np.float32)
        for sn in range(4):
            g = 4 * c + sn
            b, h = g // 16, g % 16
            for ci in range(4):
                src = r_1a[4 * b + ci]
                cols = slice(ci * TPC, (ci + 1) * TPC)
                qB[sn * 64:(sn + 1) * 64, cols] = np.asarray(src["qfs"])[h * 64:(h + 1) * 64]
                kB[sn * 64:(sn + 1) * 64, cols] = np.asarray(src["kfs"])[h * 64:(h + 1) * 64]
                ktm[sn * T + ci * TPC:sn * T + (ci + 1) * TPC] = \
                    np.asarray(src["ktms"])[:, h * 64:(h + 1) * 64]
                vtm[sn * T + ci * TPC:sn * T + (ci + 1) * TPC] = \
                    np.asarray(src["vtms"])[:, h * 64:(h + 1) * 64]
                for key, (snm, arr) in rows.items():
                    arr[sn, cols] = np.asarray(src[snm])[h]
                eL[0, sn * 64 + ci * 16:sn * 64 + (ci + 1) * 16] = \
                    np.asarray(src["eLc16"])[h]
        mp = {"qBd": qB, "kBd": kB, "ktmB": ktm, "vtmB": vtm, "eLcB": eL, "rmw64": rmw64}
        for key, (snm, arr) in rows.items():
            mp[key] = arr
        maps.append(mp)
    r_1b = _run(build_attn1b(), maps)

    W = np.asarray(d["L1_attn_W_block"], np.float32)

    def vec_reorder(arr):
        return _cc(arr.reshape(128, 2, 32, 128).transpose(0, 1, 3, 2).reshape(128, T))

    maps = []
    for c in range(NCORE):
        b = c // 4
        pr = slice((c % 4) * 128, (c % 4 + 1) * 128)
        mp = {"wbcB": _cc(np.stack([W[pr, 0, 0], W[pr, 0, 1], W[pr, 1, 0], W[pr, 1, 1]],
                                   axis=1))}
        for key, snm in (("u1B", "u1s"), ("u2B", "u2s"), ("rc1B", "rc1s"),
                         ("rc2B", "rc2s"), ("caB", "cass"), ("saB", "sass")):
            full = np.concatenate([np.asarray(r_1a[4 * b + i][snm]) for i in range(4)],
                                  axis=1)
            mp[key] = vec_reorder(full[pr])
        maps.append(mp)
    r_1c = _run(build_attn1c(), maps)

    rv = np.asarray(d["L1_attn_rms_v_w"], np.float32)
    wout_b = np.asarray(r_1a[0]["wout_b"])
    rmsv_e = _cc(rv[0::2].reshape(4, 128).T)
    rmsv_o = _cc(rv[1::2].reshape(4, 128).T)
    boutt = _cc(np.asarray(d["L1_attn_b_out"], np.float32).reshape(8, 128).T)

    def vf_toktord(arr):
        return arr.reshape(128, 2, 128, 32).transpose(0, 1, 3, 2).reshape(128, T)

    maps = []
    for c in range(NCORE):
        b = c // 4
        bcols = slice((c % 4) * TPC, (c % 4 + 1) * TPC)
        mfm = np.zeros((D, TPC), ml_dtypes.bfloat16)
        for h in range(16):
            g = b * 16 + h
            src = np.asarray(r_1b[g // 4]["mfsB"])
            mfm[h * 64:(h + 1) * 64] = src[(g % 4) * 64:(g % 4 + 1) * 64, bcols]
        vf1 = np.zeros((512, TPC), ml_dtypes.bfloat16)
        vf2 = np.zeros((512, TPC), ml_dtypes.bfloat16)
        for pi in range(4):
            vf1[pi * 128:(pi + 1) * 128] = vf_toktord(
                np.asarray(r_1c[4 * b + pi]["vf1B"]))[:, bcols]
            vf2[pi * 128:(pi + 1) * 128] = vf_toktord(
                np.asarray(r_1c[4 * b + pi]["vf2B"]))[:, bcols]
        maps.append({"mfm_d": _cc(mfm), "vf1_d": _cc(vf1), "vf2_d": _cc(vf2),
                     "gate_d": np.asarray(r_1a[c]["gate_s"]), "wout_b": wout_b,
                     "rmsv_e": rmsv_e, "rmsv_o": rmsv_o, "boutt": boutt})
    r_1d = _run(build_attn1d(), maps)

    rwt = _cc(np.asarray(d["L1_ffn_router_w"], np.float32).T)
    rb = _cc(np.asarray(d["L1_ffn_router_b"], np.float32)[None, :])
    ewt = _cc(np.asarray(d["L1_ffn_expert_w"], np.float32).transpose(0, 2, 1))
    eb = np.asarray(d["L1_ffn_expert_b"], np.float32)
    r_f1 = _run(build_ffn1(), [{"xn_fm32": np.asarray(r_rl[c]["xn_fm32"]),
                                "rwt": rwt, "rb": rb, "ewt": ewt, "eb": eb}
                               for c in range(NCORE)])

    r_o = _run(build_add3(), [{"x": np.asarray(r_rl[c]["x1_tm"]),
                               "a_tm": _cc(np.asarray(r_1d[c]["attn_fm"]).T),
                               "f_tm": np.asarray(r_f1[c]["y"])} for c in range(NCORE)])
    out = np.concatenate([np.asarray(r_o[c]["out"], np.float32) for c in range(NCORE)],
                         axis=0)
    return out.reshape(2, T, D)
